# version 18
# baseline (speedup 1.0000x reference)
from contextlib import ExitStack

import numpy as np

import concourse.bass as bass
import concourse.mybir as mybir
from concourse.bass_utils import run_bass_kernel_spmd

F32 = mybir.dt.float32
BF16 = mybir.dt.bfloat16
I32 = mybir.dt.int32
AF = mybir.ActivationFunctionType
ALU = mybir.AluOpType
AX = mybir.AxisListType.X

MAGIC = 12582912.0
LN_EPS = 1e-5
QCLIP = 1e-5
N_CORES = 8


class Prog:
    ENGS = ("sync", "scalar", "vector", "tensor", "gpsimd")

    def __init__(self, nc, es):
        self.nc = nc
        self.es = es
        self.streams = {k: [] for k in self.ENGS}
        self.sem_n = {}
        self._sems = {}

    def sem(self, name):
        if name not in self._sems:
            self._sems[name] = self.es.enter_context(self.nc.semaphore(name))
            self.sem_n[name] = 0
        return self._sems[name]

    def wait(self, eng, name, val=None):
        h = self.sem(name)
        v = self.sem_n[name] if val is None else val
        if v <= 0:
            return
        self.streams[eng].append(lambda e, h=h, v=v: e.wait_ge(h, v))

    def now(self, name):
        self.sem(name)
        return self.sem_n[name]

    _MILE = {"vector": "c_dv", "scalar": "c_sc", "tensor": "c_pe",
             "gpsimd": "c_gp", "sync": "c_sy"}

    def do(self, eng, fn, inc=None, k=1, settle=False):
        if settle and inc is None:
            inc = self._MILE[eng]
        if inc is not None:
            h = self.sem(inc)
            self.sem_n[inc] += k
            self.streams[eng].append(lambda e, fn=fn, h=h, k=k: fn(e).then_inc(h, k))
        else:
            self.streams[eng].append(lambda e, fn=fn: fn(e))
        if settle:
            self.wait(eng, inc)
        return self.sem_n.get(inc)

    def dma(self, eng, out, in_, transpose=False, slow=False, nowait=False):
        if not nowait:
            self.wait(eng, "d_" + eng)
        return self.do(
            eng,
            lambda e, o=out, i=in_, t=transpose, s=slow: e.dma_start(
                out=o, in_=i, transpose=t, allow_slow_non_contiguous=s),
            inc="d_" + eng, k=16)

    def build(self, block):
        for eng in self.ENGS:
            fns = self.streams[eng]
            if not fns:
                continue

            def body(e, fns=fns):
                for f in fns:
                    f(e)
            getattr(block, eng)(body)


def build_kernel(NSEQ, S, D, HID, HEADS, phases='faf'):
    DH = D // HEADS
    DC = D // 128
    HC = HID // 128
    TN = S // 128
    QT = S // 512
    NQ = S // 512
    SB_W = max(S, 2048)
    assert DH == 64 and D % 128 == 0 and HID % 128 == 0 and S % 512 == 0

    nc = bass.Bass("TRN2", target_bir_lowering=False,
                   detect_race_conditions=False)
    es = ExitStack()
    es.__enter__()
    p = Prog(nc, es)

    x_h = nc.dram_tensor("x", [NSEQ, S, D], F32, kind="ExternalInput").ap()
    o_h = nc.dram_tensor("out", [NSEQ, S, D], F32, kind="ExternalOutput").ap()
    wn = {}
    for nm, shp in [("qw1", [D, HID]), ("qb1", [HID]), ("qw2", [HID, D]), ("qb2", [D]),
                    ("kw1", [D, HID]), ("kb1", [HID]), ("kw2", [HID, D]), ("kb2", [D]),
                    ("vw1", [D, HID]), ("vb1", [HID]), ("vw2", [HID, D]), ("vb2", [D]),
                    ("pw", [D, D]), ("pb", [D]),
                    ("fw1", [HID, D]), ("cw", [HID, 1, 3]), ("cb", [HID]),
                    ("fw2", [D, HID])]:
        wn[nm] = nc.dram_tensor(nm, shp, F32, kind="ExternalInput").ap()

    import os as _os
    _K = "ExternalOutput" if _os.environ.get("KERNEL_DEBUG") else "Internal"
    s_w1qT = nc.dram_tensor("s_w1qT", [HID // 128, 128, D], BF16,
                            kind=_K).ap()
    NDH = max(1, D // 512)
    DHALF = D // NDH
    s_w2t4 = nc.dram_tensor("s_w2t4", [NDH, 128, HID * DHALF // 128], BF16,
                            kind=_K).ap()
    s_sil = nc.dram_tensor("s_sil", [HC, 128, S], F32, kind=_K).ap()
    s_xq2 = nc.dram_tensor("s_xq2", [HC, 128, S], BF16, kind=_K).ap()
    s_yT = nc.dram_tensor("s_yT", [D, S], F32, kind=_K).ap()
    s_q = nc.dram_tensor("s_q", [D, S], F32, kind=_K).ap()
    s_k = nc.dram_tensor("s_k", [D, S], F32, kind=_K).ap()
    s_v = nc.dram_tensor("s_v", [S, D], F32, kind=_K).ap()
    s_ao = nc.dram_tensor("s_ao", [D, S], F32, kind=_K).ap()
    s_bc = nc.dram_tensor("s_bc", [4096], F32, kind="Internal").ap()
    s_bc2 = nc.dram_tensor("s_bc2", [64], F32, kind="Internal").ap()
    s_w1t3 = nc.dram_tensor("s_w1t3", [3, HID // 128, 128, D], F32,
                            kind="Internal").ap()

    def sb(name, shape, dt):
        return es.enter_context(nc.sbuf_tensor(name, shape, dt))

    x_res = sb("x_res", [128, TN * D], F32)
    sh16k = sb("sh16k", [128, DC * S], BF16)
    gA = sb("gA", [128, SB_W], F32)
    gB = sb("gB", [128, SB_W], F32)
    gC = sb("gC", [128, SB_W], F32)
    gD = sb("gD", [128, SB_W], F32)
    gE = sb("gE", [128, SB_W], F32)
    gF = sb("gF", [128, SB_W], F32)
    gH = sb("gH", [128, 1024], F32)
    gI = sb("gI", [128, 1024], F32)
    gK = sb("gK", [128, 1024], F32)
    gL = sb("gL", [128, 1024], F32)
    gM = sb("gM", [128, 1024], F32)
    gbfA = sb("gbfA", [128, 2048], BF16)
    gXQa = sb("gXQa", [128, 4096], BF16)
    gXQb = sb("gXQb", [128, 4096], BF16)
    w1ta = sb("w1ta", [128, 1024], BF16)
    w1tb = sb("w1tb", [128, 1024], BF16)
    stat2 = sb("stat2", [128, 13 * 16], F32)
    stat3 = sb("stat3", [128, 64], F32)
    cwc = sb("cwc", [128, 4 * HC], F32)
    b1col = sb("b1col", [128, 3 * HC], F32)
    b2col = sb("b2col", [128, 3 * DC], F32)
    pbbc = sb("pbbc", [128, 1024], F32)
    b2bc = sb("b2bc", [128, 1024], F32)
    ones_r = sb("ones_r", [1, 128], F32)
    sm_r = sb("sm_r", [1, 2048], F32)
    csc = sb("csc", [128, 8], F32)
    identb = sb("identb", [128, 128], BF16)
    identf = sb("identf", [128, 128], F32)

    PS = [es.enter_context(nc.psum_tensor(f"ps{i}", [128, 512], F32)) for i in range(6)]
    PT = es.enter_context(nc.psum_tensor("pt", [128, 1024], BF16))
    P7 = es.enter_context(nc.psum_tensor("p7", [128, 512], F32))

    SLOT = {"msum": 0, "mx": 1, "mn": 2, "ssq": 3, "m": 4, "var": 5, "rstd": 6,
            "amaxc": 7, "sprime": 8, "biasra": 9, "inva": 10, "s2tok": 11,
            "lnb": 12}

    def st(name, i):
        b = SLOT[name]
        return stat2[:, b * 16 + i: b * 16 + i + 1]

    def stb(name, n=None):
        b = SLOT[name]
        n = TN if n is None else n
        return stat2[:, b * 16: b * 16 + n]

    def x_tile(t):
        return x_res[:, t * D:(t + 1) * D]

    V = lambda fn, inc=None, settle=True: p.do("vector", fn, inc=inc, settle=settle)
    SC = lambda fn, inc=None, settle=False: p.do("scalar", fn, inc=inc, settle=settle)
    TE = lambda fn, inc=None: p.do("tensor", fn, inc=inc)
    GP = lambda fn, inc=None: p.do("gpsimd", fn, inc=inc)

    def TS1(out, in0, scalar, op):
        return lambda e, o=out, i=in0, s=scalar, op=op: e.tensor_single_scalar(
            out=o, in_=i, scalar=s, op=op)

    def TS2(out, in0, s1, s2, op0, op1):
        return lambda e, o=out, i=in0, a=s1, b=s2, op0=op0, op1=op1: e.tensor_scalar(
            out=o, in0=i, scalar1=a, scalar2=b, op0=op0, op1=op1)

    def TT(out, in0, in1, op):
        return lambda e, o=out, a=in0, b=in1, op=op: e.tensor_tensor(
            out=o, in0=a, in1=b, op=op)

    GP(lambda e: e.memset(identb[:], 0.0), inc="c_gp")
    GP(lambda e: e.memset(identf[:], 0.0), inc="c_gp")
    GP(lambda e: e.memset(ones_r[:], 1.0), inc="c_gp")
    GP(lambda e: e.memset(csc[:, 5:6], MAGIC), inc="c_gp")
    GP(lambda e: e.memset(csc[:, 6:7], LN_EPS), inc="c_gp")
    p.wait("gpsimd", "c_gp")
    GP(lambda e: e.affine_select(out=identb[:], in_=identb[:],
                                 compare_op=ALU.not_equal, fill=1.0, base=0,
                                 pattern=[[-1, 128]], channel_multiplier=1),
       inc="c_gp")
    GP(lambda e: e.affine_select(out=identf[:], in_=identf[:],
                                 compare_op=ALU.not_equal, fill=1.0, base=0,
                                 pattern=[[-1, 128]], channel_multiplier=1),
       inc="c_gp")
    p.wait("tensor", "c_gp")
    p.wait("scalar", "c_gp")
    p.wait("vector", "c_gp")

    cw_r = wn["cw"].rearrange("(c p) one t -> p (one t) c", p=128)
    for t in range(3):
        p.dma("scalar", cwc[:, t * HC:(t + 1) * HC], cw_r[:, t, :], slow=True)
    p.dma("scalar", cwc[:, 3 * HC:4 * HC],
          wn["cb"].rearrange("(c p) -> p c", p=128), slow=True)
    for i, nm in enumerate(("qb1", "kb1", "vb1")):
        p.dma("scalar", b1col[:, i * HC:(i + 1) * HC],
              wn[nm].rearrange("(c p) -> p c", p=128), slow=True)
    for i, nm in enumerate(("qb2", "kb2", "vb2")):
        p.dma("scalar", b2col[:, i * DC:(i + 1) * DC],
              wn[nm].rearrange("(c p) -> p c", p=128), slow=True)
    p.dma("scalar", pbbc[:, 0:D], wn["pb"][None, :].broadcast_to((128, D)))
    p.dma("scalar", b2bc[:, 0:D], wn["vb2"][None, :].broadcast_to((128, D)))
    consts_dv = p.now("d_scalar")

    def quant_ffn_w(w_ap, n_rows, n_cols, s_out, invw_slot, sw_slot,
                    tiled_out=False):
        RT = n_rows // 128
        CW = min(n_cols, 2048)
        NCB = n_cols // CW
        w_r = w_ap.rearrange("(r p) c -> r p c", p=128)
        V(lambda e: e.memset(stat3[:, 1:2], 0.0), inc="c_dv")
        for r in range(RT):
            for cb in range(NCB):
                p.wait("sync", "c_dv")
                dv = p.dma("sync", gA[:, 0:CW], w_r[r][:, cb * CW:(cb + 1) * CW])
                p.wait("vector", "d_sync", dv)
                V(lambda e: e.tensor_reduce(out=stat3[:, 0:1], in_=gA[:, 0:CW],
                                            axis=AX, op=ALU.add,
                                            apply_absolute_value=True), settle=True)
                V(TT(stat3[:, 1:2], stat3[:, 1:2], stat3[:, 0:1], ALU.add), inc="c_dv")
        V(lambda e: e.memset(stat3[:, 2:3], 1.0), inc="c_dv")
        p.wait("tensor", "c_dv")
        TE(lambda e: e.matmul(PS[0][0:1, 0:1], stat3[:, 1:2], stat3[:, 2:3],
                              start=True, stop=True), inc="c_pe")
        p.wait("vector", "c_pe")
        V(TS2(stat3[0:1, 3:4], PS[0][0:1, 0:1], 1.0 / (n_rows * n_cols), QCLIP,
              ALU.mult, ALU.max))
        V(lambda e: e.reciprocal(stat3[0:1, 4:5], stat3[0:1, 3:4]), inc="c_dv",
          settle=True)
        p.wait("gpsimd", "c_dv")
        p.wait("gpsimd", "d_scalar")
        p.dma("gpsimd", s_bc2[0:2], stat3[0:1, 3:5])
        p.wait("scalar", "d_gpsimd")
        p.dma("scalar", csc[:, invw_slot:invw_slot + 1],
              s_bc2[0:1].broadcast_to((128, 1)))
        sw_ready = p.dma("scalar", csc[:, sw_slot:sw_slot + 1],
                         s_bc2[1:2].broadcast_to((128, 1)))
        p.wait("sync", "c_dv")
        for r in range(RT):
            for cb in range(NCB):
                p.wait("sync", "c_sc")
                dv = p.dma("sync", gA[:, 0:CW], w_r[r][:, cb * CW:(cb + 1) * CW])
                p.wait("scalar", "d_sync", dv)
                p.wait("scalar", "d_scalar", sw_ready)
                p.wait("scalar", "c_dv")
                SC(lambda e: e.activation(
                    out=gB[:, 0:CW], in_=gA[:, 0:CW], func=AF.Identity,
                    bias=csc[:, 5:6], scale=csc[:, sw_slot:sw_slot + 1]), inc="c_sc")
                p.wait("vector", "c_sc")
                V(TS1(gB[:, 0:CW].bitcast(I32), gB[:, 0:CW], MAGIC, ALU.subtract),
                  settle=False)
                V(TS2(gbfA[:, 0:CW], gB[:, 0:CW].bitcast(I32), 1.0, -1.0,
                      ALU.min, ALU.max), inc="c_dv", settle=False)
                p.wait("tensor", "c_dv")
                p.wait("vector", "d_gpsimd")
                CT = CW // 128
                for c in range(CT):
                    last = (c % 8 == 7 or c == CT - 1)
                    TE(lambda e, c=c: e.transpose(
                        PT[:, (c % 8) * 128:(c % 8) * 128 + 128],
                        gbfA[:, c * 128:(c + 1) * 128], identb[:]),
                       inc=("c_pe" if last else None))
                    if last:
                        n8 = (c % 8) + 1
                        p.wait("vector", "c_pe")
                        V(lambda e, c=c, n8=n8: e.tensor_copy(
                            gXQa[:, (c - n8 + 1) * 128:(c + 1) * 128],
                            PT[:, 0:n8 * 128]), inc="c_dv", settle=False)
                        p.wait("tensor", "c_dv")
                p.wait("gpsimd", "c_dv")
                if tiled_out:
                    p.dma("gpsimd", s_out[r], gXQa[:, 0:CT * 128])
                else:
                    off = (r * 128) % DHALF
                    out_ap = s_out[(r * 128) // DHALF].rearrange(
                        "p (c u) -> p c u", u=DHALF)[:, cb * CT:(cb + 1) * CT,
                                                     off:off + 128]
                    p.dma("gpsimd", out_ap,
                          gXQa[:, 0:CT * 128].rearrange("p (c u) -> p c u", u=128))

    quant_ffn_w(wn["fw1"], HID, D, s_w1qT, 0, 1, tiled_out=True)
    w1qT_dv = p.now("d_gpsimd")

    quant_ffn_w(wn["fw2"], D, HID, s_w2t4, 2, 3)
    w2qT_dv = p.now("d_gpsimd")

    for wi, w1n in enumerate(("qw1", "kw1", "vw1")):
        w1s = wn[w1n].rearrange("(c p) h -> p c h", p=128)
        for hc in range(HC):
            p.dma("scalar",
                  s_w1t3[wi, hc].rearrange("p (c u) -> p c u", u=128),
                  w1s[:, :, hc * 128:(hc + 1) * 128], slow=True, nowait=True)
    w1t3_dv = p.now("d_scalar")

    def ln_stats(quant):
        p.wait("scalar", "c_dv")
        for t in range(TN):
            xt = x_tile(t)
            V(lambda e, xt=xt, t=t: e.tensor_reduce(out=st("msum", t), in_=xt,
                                                    axis=AX, op=ALU.add),
              inc="c_dv", settle=False)
            if quant:
                V(lambda e, xt=xt, t=t: e.tensor_reduce(out=st("mx", t), in_=xt,
                                                        axis=AX, op=ALU.max),
                  inc="c_dv", settle=False)
                V(lambda e, xt=xt, t=t: e.tensor_reduce(out=st("mn", t), in_=xt,
                                                        axis=AX, op=ALU.min),
                  inc="c_dv", settle=False)
            SC(lambda e, xt=xt, t=t: e.activation(out=gH[:, 0:D], in_=xt,
                                                  func=AF.Square, bias=0.0, scale=1.0,
                                                  accum_out=st("ssq", t)), inc="c_sc")
        p.wait("vector", "c_dv")
        p.wait("vector", "c_sc")
        V(TS1(stb("m"), stb("msum"), 1.0 / D, ALU.mult))
        V(TT(stb("var"), stb("m"), stb("m"), ALU.mult))
        V(TS1(stb("ssq"), stb("ssq"), 1.0 / D, ALU.mult))
        V(TT(stb("var"), stb("ssq"), stb("var"), ALU.subtract), inc="c_dv")
        p.wait("scalar", "c_dv")
        SC(lambda e: e.activation(out=stb("rstd"), in_=stb("var"), func=AF.Sqrt,
                                  bias=csc[:, 6:7], scale=1.0), inc="c_sc")
        p.wait("vector", "c_sc")
        V(lambda e: e.reciprocal(stb("rstd"), stb("rstd")), inc="c_dv",
          settle=True)

    def bcast_block(src_blk, n, dst, width):
        p.wait("tensor", "c_dv")
        TE(lambda e: e.transpose(P7[0:n, 0:128], src_blk, identf[:]), inc="c_pe")
        p.wait("vector", "c_pe")
        p.wait("vector", "d_gpsimd")
        V(lambda e, n=n: e.tensor_copy(gH[0:n, 0:128], P7[0:n, 0:128]), inc="c_dv")
        p.wait("gpsimd", "c_dv")
        p.dma("gpsimd", s_bc[0:width], gH[0:n, 0:128])
        p.wait("gpsimd", "c_dv")
        return p.dma("gpsimd", dst, s_bc[None, 0:width].broadcast_to((128, width)))

    def ffn(seq):
        ln_stats(quant=True)
        V(TT(stb("mx"), stb("mx"), stb("m"), ALU.subtract))
        V(TT(stb("mn"), stb("m"), stb("mn"), ALU.subtract))
        V(TT(stb("mx"), stb("mx"), stb("mn"), ALU.max))
        V(TT(stb("amaxc"), stb("mx"), stb("rstd"), ALU.mult))
        V(TS1(stb("amaxc"), stb("amaxc"), QCLIP, ALU.max), inc="c_dv")
        p.wait("vector", "c_dv")
        V(lambda e: e.reciprocal(stb("sprime"), stb("amaxc")), settle=True)
        V(TS1(stb("sprime"), stb("sprime"), 127.0, ALU.mult))
        V(TT(stb("sprime"), stb("sprime"), stb("rstd"), ALU.mult))
        V(TT(stb("biasra"), stb("m"), stb("sprime"), ALU.mult))
        V(TS1(stb("biasra"), stb("biasra"), -1.0, ALU.mult))
        V(TS1(stb("inva"), stb("amaxc"), 1.0 / 127.0, ALU.mult))
        V(TS1(stb("inva"), stb("inva"), csc[:, 0:1], ALU.mult), inc="c_dv")

        for t in range(TN):
            xt = x_tile(t)
            p.wait("scalar", "c_dv")
            SC(lambda e, xt=xt, t=t: e.activation(out=gH[:, 0:D], in_=xt, func=AF.Identity,
                                                  bias=st("biasra", t),
                                                  scale=st("sprime", t)), inc="c_sc")
            p.wait("vector", "c_sc")
            V(TS2(gbfA[:, 0:D], gH[:, 0:D], MAGIC, MAGIC,
                  ALU.add, ALU.subtract), inc="c_dv", settle=False)
            p.wait("tensor", "c_dv")
            for dc in range(DC):
                TE(lambda e, dc=dc: e.transpose(PT[:, dc * 128:(dc + 1) * 128],
                                                gbfA[:, dc * 128:(dc + 1) * 128],
                                                identb[:]),
                   inc=("c_pe" if dc == DC - 1 else None))
            p.wait("vector", "c_pe")
            V(lambda e, t=t: e.tensor_copy(
                sh16k[:].rearrange("p (c s) -> p c s", c=DC)[:, :, t * 128:(t + 1) * 128],
                PT[:, 0:DC * 128].rearrange("p (c u) -> p c u", c=DC)), inc="c_dv",
              settle=False)
            p.wait("tensor", "c_dv")
        xqT_done = p.now("c_dv")

        inva_dv = bcast_block(stb("inva"), TN, gF[:, 0:S], S)

        p.wait("vector", "d_gpsimd", inva_dv)
        p.wait("vector", "c_gp")
        V(lambda e: e.memset(gE[:, 0:S], 0.0), inc="c_gp", settle=False)
        sil_bufs = (gC, gD)
        w1_bufs = (w1ta, w1tb)
        ga_ping = (gA, gXQa.bitcast(F32))
        w1r = s_w1qT
        pe_done = {}
        spill_done = {}
        evac_done = {}
        center_done = {}
        c_done = {}
        silu_done = {}
        abs_done = {}
        fold_done = {}
        p.wait("sync", "d_gpsimd", w1qT_dv)
        for hc in range(HC + 2):
            if hc < HC:
                wb = w1_bufs[hc % 2]
                ga = ga_ping[hc % 2]
                if hc >= 2:
                    p.wait("sync", "c_pe", pe_done[hc - 2])
                dv = p.dma("sync", wb[:, 0:DC * 128], w1r[hc])
                p.wait("tensor", "d_sync", dv)
                if hc == 0:
                    p.wait("tensor", "c_dv", xqT_done)
                if hc >= 2:
                    p.wait("vector", "c_sc", center_done[hc - 2])
                for q in range(NQ):
                    gi = hc * NQ + q
                    ps = PS[gi % 2]
                    if gi - 2 in evac_done:
                        p.wait("tensor", "c_dv", evac_done[gi - 2])
                    for dc in range(DC):
                        TE(lambda e, ps=ps, wb=wb, dc=dc, q=q: e.matmul(
                            ps[:, 0:512], wb[:, dc * 128:(dc + 1) * 128],
                            sh16k[:, dc * S + q * 512: dc * S + (q + 1) * 512],
                            start=(dc == 0), stop=(dc == DC - 1)),
                           inc=("c_pe" if dc == DC - 1 else None))
                    p.wait("vector", "c_pe")
                    evac_done[gi] = V(
                        TT(ga[:, q * 512:(q + 1) * 512], ps[:, 0:512],
                           gF[:, q * 512:(q + 1) * 512], ALU.mult), inc="c_dv",
                        settle=False)
                pe_done[hc] = p.now("c_pe")
            if 2 <= hc and hc - 2 < HC:
                f = hc - 2
                p.wait("vector", "c_sc", abs_done[f])
                fold_done[f] = V(TT(gE[:, 0:S], gE[:, 0:S], gB[:, 0:S], ALU.max),
                                 inc="c_gp", settle=False)
            if 1 <= hc and hc - 1 < HC:
                c = hc - 1
                ga = ga_ping[c % 2]
                p.wait("scalar", "c_dv", evac_done[c * NQ + NQ - 1])
                if c >= 1:
                    p.wait("scalar", "c_gp", fold_done[c - 1])
                SC(lambda e, c=c, ga=ga: e.activation(
                    out=gB[:, 0:S], in_=ga[:, 0:S], func=AF.Identity,
                    bias=cwc[:, 3 * HC + c:3 * HC + c + 1],
                    scale=cwc[:, HC + c:HC + c + 1]), inc="c_sc")
                center_done[c] = p.now("c_sc")
                p.wait("vector", "c_sc", center_done[c])
                V(lambda e, c=c, ga=ga: e.scalar_tensor_tensor(
                    out=gB[:, 1:S], in0=ga[:, 0:S - 1], scalar=cwc[:, c:c + 1],
                    in1=gB[:, 1:S], op0=ALU.mult, op1=ALU.add), settle=False)
                c_done[c] = V(lambda e, c=c, ga=ga: e.scalar_tensor_tensor(
                    out=gB[:, 0:S - 1], in0=ga[:, 1:S],
                    scalar=cwc[:, 2 * HC + c:2 * HC + c + 1],
                    in1=gB[:, 0:S - 1], op0=ALU.mult, op1=ALU.add), inc="c_dv",
                    settle=False)
                sil = sil_bufs[c % 2]
                if c >= 2:
                    p.wait("scalar", "d_gpsimd", spill_done[c - 2])
                p.wait("scalar", "c_dv", c_done[c])
                silu_done[c] = SC(lambda e, sil=sil: e.activation(
                    out=sil[:, 0:S], in_=gB[:, 0:S], func=AF.Silu,
                    bias=0.0, scale=1.0), inc="c_sc")
                abs_done[c] = SC(lambda e, sil=sil: e.activation(
                    out=gB[:, 0:S], in_=sil[:, 0:S], func=AF.Abs,
                    bias=0.0, scale=1.0), inc="c_sc")
                p.wait("gpsimd", "c_sc", silu_done[c])
                spill_done[c] = p.dma("gpsimd", s_sil[c], sil[:, 0:S])

        p.wait("tensor", "c_dv")
        p.wait("tensor", "c_gp")
        for i in range(TN):
            TE(lambda e, i=i: e.transpose(PS[i % 4][:, 0:128],
                                          gE[:, i * 128:(i + 1) * 128], identf[:]),
               inc="c_pe")
            p.wait("vector", "c_pe")
            V(lambda e, i=i: e.tensor_reduce(out=st("mx", i), in_=PS[i % 4][:, 0:128],
                                             axis=AX, op=ALU.max), inc="c_dv",
              settle=False)
            if i >= 3:
                p.wait("tensor", "c_dv", p.now("c_dv") - 3)
        p.wait("vector", "c_dv")
        V(TS1(stb("mx"), stb("mx"), QCLIP, ALU.max), inc="c_dv")
        p.wait("vector", "c_dv")
        V(lambda e: e.reciprocal(stb("amaxc"), stb("mx")), settle=True)
        V(TS1(stb("sprime"), stb("amaxc"), 127.0, ALU.mult))
        V(TS2(stb("s2tok"), stb("mx"), 1.0 / 127.0, csc[:, 2:3], ALU.mult, ALU.mult))
        V(TS1(stb("s2tok"), stb("s2tok"), 0.5, ALU.mult), inc="c_dv")
        s2_dv = bcast_block(stb("sprime"), TN, gF[:, 0:S], S)

        p.wait("vector", "d_gpsimd", s2_dv)
        p.wait("sync", "d_gpsimd", spill_done[HC - 1])
        p.wait("sync", "c_sc")
        xb_ping = (gI.bitcast(BF16), gM.bitcast(BF16))
        use_done = {}
        xq2_spill = {}
        for hc in range(HC):
            buf = sil_bufs[hc % 2]
            xbuf = xb_ping[hc % 2]
            if hc >= 2:
                p.wait("sync", "c_dv", use_done[hc - 2])
            dv = p.dma("sync", buf[:, 0:S], s_sil[hc])
            p.wait("vector", "d_sync", dv)
            if hc >= 2:
                p.wait("vector", "d_gpsimd", xq2_spill[hc - 2])
            V(TT(gB[:, 0:S], buf[:, 0:S], gF[:, 0:S], ALU.mult), settle=False)
            use_done[hc] = p.now("c_dv") + 1
            V(TS2(xbuf[:, 0:S], gB[:, 0:S], MAGIC, MAGIC,
                  ALU.add, ALU.subtract), inc="c_dv", settle=False)
            p.wait("gpsimd", "c_dv")
            xq2_spill[hc] = p.dma("gpsimd", s_xq2[hc], xbuf[:, 0:S])

        poolA = ((gXQa, 32), (gbfA, 16), (gH.bitcast(BF16), 16))
        poolB = ((gXQb, 32), (gK.bitcast(BF16), 16), (gL.bitcast(BF16), 16))

        def xq_tile(pool, t):
            for buf, cap in pool:
                if t < cap:
                    return buf[:, t * 128:(t + 1) * 128]
                t -= cap
            raise AssertionError

        def xq_range(pool, t0, t1):
            for buf, cap in pool:
                if t0 < cap:
                    assert t1 <= cap
                    return buf[:, t0 * 128:t1 * 128]
                t0 -= cap
                t1 -= cap
            raise AssertionError

        chunks = []
        base_hc = 0
        for _, cap in poolA:
            span = cap // 2
            lo = 0
            while lo < span and base_hc + lo < HC:
                hi = min(lo + 6, span, HC - base_hc)
                chunks.append((base_hc + lo, base_hc + hi))
                lo = hi
            base_hc += span
        qassign = {q: [] for q in ("sync", "scalar", "gpsimd")}
        for lo, hi in sorted(chunks, key=lambda c: c[0] - c[1]):
            q = min(qassign, key=lambda k: sum(b - a for a, b in qassign[k]))
            qassign[q].append((lo, hi))

        sx = s_xq2.rearrange("c p s -> p c s")
        p.wait("sync", "d_gpsimd", w2qT_dv)
        mm_done = {}
        stt_done = {}
        NG2 = (TN + 1) // 2
        for half in range(NDH):
            p.wait("sync", "c_pe")
            dv = p.dma("sync", sh16k[:, 0:HC * DHALF], s_w2t4[half])
            p.wait("tensor", "d_sync", dv)
            for g2 in range(NG2):
                pool = (poolA, poolB)[g2 % 2]
                gs = [g for g in (g2 * 2, g2 * 2 + 1) if g < TN]
                nu = len(gs) * 128
                guard = mm_done.get(half * NG2 + g2 - 2)
                dvqs = {}
                hc_dv = {}
                for qeng, rr in qassign.items():
                    if guard is not None:
                        p.wait(qeng, "c_pe", guard)
                    for lo, hi in rr:
                        p.wait(qeng, "d_gpsimd", xq2_spill[hi - 1])
                        dvqs[qeng] = p.dma(
                            qeng,
                            xq_range(pool, lo * 2, hi * 2).rearrange(
                                "p (c u) -> p c u", u=nu),
                            sx[:, lo:hi, g2 * nu:(g2 + 1) * nu])
                        for h_ in range(lo, hi):
                            hc_dv[h_] = (qeng, dvqs[qeng])
                waited = set()
                for g in gs:
                    sub = g % 2
                    NW = min(DHALF, 512)
                    for n in range(max(1, DHALF // 512)):
                        psi = 2 + 2 * (g % 2) + (n % 2)
                        ps = PS[psi]
                        if psi in stt_done:
                            p.wait("tensor", "c_dv", stt_done[psi])
                        for hcc in range(HC):
                            qeng, dvq = hc_dv[hcc]
                            if (qeng, dvq) not in waited:
                                p.wait("tensor", "d_" + qeng, dvq)
                                waited.add((qeng, dvq))
                            xt_ = xq_tile(pool, hcc * 2 + sub)
                            TE(lambda e, ps=ps, xt_=xt_, hcc=hcc, n=n, NW=NW:
                               e.matmul(
                                   ps[:, 0:NW], xt_,
                                   sh16k[:, hcc * DHALF + n * 512:
                                         hcc * DHALF + n * 512 + NW],
                                   start=(hcc == 0), stop=(hcc == HC - 1)),
                               inc=("c_pe" if hcc == HC - 1 else None))
                        p.wait("vector", "c_pe")
                        xsl = x_tile(g)[:, half * DHALF + n * 512:
                                        half * DHALF + n * 512 + NW]
                        stt_done[psi] = V(lambda e, ps=ps, xsl=xsl, g=g, NW=NW:
                          e.scalar_tensor_tensor(
                              out=xsl, in0=ps[:, 0:NW], scalar=st("s2tok", g),
                              in1=xsl, op0=ALU.mult, op1=ALU.add), inc="c_dv",
                          settle=False)
                mm_done[half * NG2 + g2] = p.now("c_pe")

    def attn(seq):
        ln_stats(quant=False)
        V(TT(stb("lnb"), stb("m"), stb("rstd"), ALU.mult))
        V(TS1(stb("lnb"), stb("lnb"), -1.0, ALU.mult), inc="c_dv")
        y_spill = 0
        for t in range(TN):
            xt = x_tile(t)
            p.wait("scalar", "c_dv")
            p.wait("scalar", "c_pe")
            SC(lambda e, xt=xt, t=t: e.activation(out=gH[:, 0:D], in_=xt, func=AF.Identity,
                                                  bias=st("lnb", t),
                                                  scale=st("rstd", t)), inc="c_sc")
            p.wait("tensor", "c_sc")
            for dc in range(DC):
                TE(lambda e, dc=dc: e.transpose(
                    PS[dc // 4][:, (dc % 4) * 128:(dc % 4) * 128 + 128],
                    gH[:, dc * 128:(dc + 1) * 128], identf[:]),
                   inc=("c_pe" if dc == DC - 1 else None))
            p.wait("vector", "c_pe")
            if t > 0:
                p.wait("vector", "d_gpsimd", y_spill)
            else:
                p.wait("vector", "d_gpsimd")
            for b in range((DC + 3) // 4):
                w = min(4, DC - b * 4) * 128
                V(lambda e, b=b, w=w: e.tensor_copy(gI[:, b * 512:b * 512 + w],
                                                    PS[b][:, 0:w]), inc="c_dv",
                  settle=False)
            p.wait("tensor", "c_dv")
            p.wait("gpsimd", "c_dv")
            out_ap = s_yT.rearrange("(c p) s -> p c s", p=128)[:, :, t * 128:(t + 1) * 128]
            y_spill = p.dma("gpsimd", out_ap,
                            gI[:, 0:DC * 128].rearrange("p (c u) -> p c u", c=DC))
        yT_dv = p.now("d_gpsimd")

        TT_ = 256
        NT = S // TT_
        w_bufs = ((gH, gK), (gI, gL), (gC, gD))
        hs_off = (0, 512, 1024)
        yb_bufs = (gF, gE)
        mm_hist = {}
        gelu_done = {}
        w2_dv = {}
        evac_sc = {}
        evac_dv = {}
        ghc = 0
        for pi, (w1n, w2n, s_dst, fm) in enumerate(
                (("qw1", "qw2", s_q, True), ("kw1", "kw2", s_k, True),
                 ("vw1", "vw2", s_v, False))):
            w1_ap = s_w1t3[pi]
            w2_ap = wn[w2n].rearrange("(c p) d -> c p d", p=128)

            def emit_mm2(j, fm=fm):
                ti_j, hcl = j // HC, j % HC
                wb2 = w_bufs[j % 3][1]
                hbuf = gA[:, hs_off[j % 3]:hs_off[j % 3] + TT_]
                p.wait("tensor", "c_sc", gelu_done[j])
                p.wait("tensor", "d_scalar", w2_dv[j])
                if hcl == 0 and ti_j >= 1:
                    if ti_j - 1 in evac_sc:
                        p.wait("tensor", "c_sc", evac_sc[ti_j - 1])
                    else:
                        p.wait("tensor", "c_dv", evac_dv[ti_j - 1])
                if fm:
                    for dc in range(DC):
                        psq = PS[2 + dc // 2]
                        off = (dc % 2) * TT_
                        TE(lambda e, psq=psq, off=off, wb2=wb2, dc=dc, hbuf=hbuf,
                           hcl=hcl: e.matmul(
                               psq[:, off:off + TT_],
                               wb2[:, dc * 128:(dc + 1) * 128], hbuf,
                               start=(hcl == 0 and dc % 2 == 0),
                               stop=(hcl == HC - 1 and dc % 2 == 1)),
                           inc=("c_pe" if dc == DC - 1 else None))
                else:
                    nd = max(1, D // 512)
                    NW = min(D, 512)
                    for g in range(TT_ // 128):
                        for n in range(nd):
                            psq = PS[2 + g * nd + n]
                            TE(lambda e, psq=psq, NW=NW, hbuf=hbuf, g=g, wb2=wb2,
                               n=n, hcl=hcl: e.matmul(
                                   psq[:, 0:NW], hbuf[:, g * 128:(g + 1) * 128],
                                   wb2[:, n * 512:n * 512 + NW],
                                   start=(hcl == 0), stop=(hcl == HC - 1)),
                               inc=("c_pe" if (g == TT_ // 128 - 1 and
                                               n == nd - 1) else None))
                mm_hist[j] = p.now("c_pe")

            for tt in range(NT):
                ti = pi * NT + tt
                yb = yb_bufs[ti % 2]
                if ti <= 1:
                    p.wait("sync", "c_pe")
                    if ti == 0:
                        p.wait("sync", "c_dv")
                        p.wait("sync", "d_gpsimd", yT_dv)
                        p.wait("sync", "d_scalar", w1t3_dv)
                else:
                    p.wait("sync", "c_pe", mm_hist[(ti - 2) * HC + HC - 1])
                p.dma("sync", yb[:, 0:DC * TT_].rearrange("p (c u) -> p c u", c=DC),
                      s_yT.rearrange("(c p) s -> p c s", p=128)[:, :,
                                                               tt * TT_:(tt + 1) * TT_])
                for hc in range(HC):
                    wb1, wb2 = w_bufs[ghc % 3]
                    if ghc >= 3:
                        p.wait("sync", "c_pe", mm_hist[ghc - 3])
                        p.wait("scalar", "c_pe", mm_hist[ghc - 3])
                    dv1 = p.dma("sync", wb1[:, 0:DC * 128], w1_ap[hc])
                    w2_dv[ghc] = p.dma("scalar", wb2[:, 0:D], w2_ap[hc])
                    p.wait("tensor", "d_sync", dv1)
                    ps1 = PS[ghc % 2]
                    if ghc >= 2:
                        p.wait("tensor", "c_sc", gelu_done[ghc - 2])
                    for dc in range(DC):
                        TE(lambda e, ps1=ps1, wb1=wb1, dc=dc, yb=yb: e.matmul(
                            ps1[:, 0:TT_], wb1[:, dc * 128:(dc + 1) * 128],
                            yb[:, dc * TT_:(dc + 1) * TT_],
                            start=(dc == 0), stop=(dc == DC - 1)),
                           inc=("c_pe" if dc == DC - 1 else None))
                    p.wait("scalar", "c_pe")
                    hbuf = gA[:, hs_off[ghc % 3]:hs_off[ghc % 3] + TT_]
                    gelu_done[ghc] = SC(
                        lambda e, ps1=ps1, hbuf=hbuf, pi=pi, hc=hc: e.activation(
                            out=hbuf, in_=ps1[:, 0:TT_], func=AF.Gelu,
                            bias=b1col[:, pi * HC + hc:pi * HC + hc + 1], scale=1.0),
                        inc="c_sc")
                    if hc >= 1:
                        emit_mm2(ghc - 1)
                    ghc += 1
                emit_mm2(ghc - 1)
                p.wait("scalar", "c_pe")
                if fm:
                    p.wait("scalar", "d_gpsimd")
                    p.wait("scalar", "c_dv")
                    for dc in range(DC):
                        psq = PS[2 + dc // 2]
                        off = (dc % 2) * TT_
                        SC(lambda e, psq=psq, off=off, dc=dc, pi=pi, tt=tt: e.activation(
                            out=gB[:, dc * TT_:(dc + 1) * TT_], in_=psq[:, off:off + TT_],
                            func=AF.Identity, bias=b2col[:, pi * DC + dc:pi * DC + dc + 1],
                            scale=1.0), inc="c_sc")
                    p.wait("vector", "c_sc")
                    V(lambda e, tt=tt: e.tensor_reduce(
                        out=stat3[:, 32 + tt:33 + tt], in_=gB[:, 0:DC * TT_], axis=AX,
                        op=ALU.max, apply_absolute_value=True), inc="c_dv",
                      settle=True)
                    p.wait("gpsimd", "c_dv")
                    out_ap = s_dst.rearrange("(c p) s -> p c s", p=128)[:, :,
                                                                        tt * TT_:(tt + 1) * TT_]
                    p.dma("gpsimd", out_ap,
                          gB[:, 0:DC * TT_].rearrange("p (c u) -> p c u", c=DC))
                    evac_sc[ti] = p.now("c_sc")
                else:
                    nd = max(1, D // 512)
                    NW = min(D, 512)
                    p.wait("vector", "c_pe")
                    for g in range(TT_ // 128):
                        p.wait("vector", "d_gpsimd")
                        for n in range(nd):
                            psq = PS[2 + g * nd + n]
                            V(TT(gM[:, n * 512:n * 512 + NW], psq[:, 0:NW],
                                 b2bc[:, n * 512:n * 512 + NW], ALU.add),
                              inc=("c_dv" if n == nd - 1 else None), settle=False)
                        p.wait("gpsimd", "c_dv")
                        p.dma("gpsimd",
                              s_v[tt * TT_ + g * 128: tt * TT_ + (g + 1) * 128, :],
                              gM[:, 0:D])
                    V(lambda e: e.engine_nop(), inc="c_dv")
                    evac_dv[ti] = p.now("c_dv")
            if fm:
                V(lambda e, pi=pi, NT=NT: e.tensor_reduce(
                    out=stat3[:, 48 + pi:49 + pi], in_=stat3[:, 32:32 + NT], axis=AX,
                    op=ALU.max), inc="c_dv", settle=True)
        qkv_spill = p.now("d_gpsimd")

        V(TT(stat3[:, 50:51], stat3[:, 48:49], stat3[:, 49:50], ALU.max),
          inc="c_dv")
        p.wait("tensor", "c_dv")
        TE(lambda e: e.transpose(P7[0:1, 0:128], stat3[:, 50:51], identf[:]),
           inc="c_pe")
        p.wait("vector", "c_pe")
        V(lambda e: e.tensor_reduce(out=gH[0:1, 0:1], in_=P7[0:1, 0:128], axis=AX,
                                    op=ALU.max), settle=True)
        V(TT(gH[0:1, 1:2], gH[0:1, 0:1], gH[0:1, 0:1], ALU.mult))
        V(TS2(gH[0:1, 1:2], gH[0:1, 1:2], 8.0, 80.0, ALU.mult, ALU.min))
        V(TS1(gH[0:1, 1:2], gH[0:1, 1:2], -1.0, ALU.mult), inc="c_dv")
        p.wait("gpsimd", "c_dv")
        p.dma("gpsimd", s_bc2[0:1], gH[0:1, 1:2])
        cb_dv = p.dma("gpsimd", csc[:, 4:5], s_bc2[0:1].broadcast_to((128, 1)))

        TNK = S // 128
        for buf in (gC, gD):
            V(lambda e, buf=buf: e.memset(
                buf[:, 0:TNK * 65].rearrange("p (k c) -> p k c", c=65)[:, :, 64:65],
                1.0), inc="c_dv")
        p.wait("scalar", "d_gpsimd", cb_dv)
        p.wait("sync", "c_dv")
        p.wait("sync", "c_pe")
        p.wait("sync", "d_gpsimd", qkv_spill)
        head_done = {}
        shv = sh16k[:].bitcast(F32)
        exp_done = {}
        av_free = {}
        gkc = 0
        for h in range(HEADS):
            vb = (gC, gD)[h % 2]
            if h % 2 == 0:
                kfm, qfm = gA, gB
            else:
                kfm, qfm = gF, shv
            if h >= 2:
                p.wait("sync", "c_pe", head_done[h - 2])
            dvk = p.dma("sync", kfm[0:64, 0:S], s_k[h * 64:(h + 1) * 64, :])
            dvq = p.dma("sync", qfm[0:64, 0:S], s_q[h * 64:(h + 1) * 64, :])
            dvv = p.dma("sync",
                        vb[:, 0:TNK * 65].rearrange("p (k c) -> p k c", c=65)[:, :, 0:64],
                        s_v.rearrange("(k p) d -> p k d", p=128)[:, :,
                                                                 h * 64:(h + 1) * 64])
            p.wait("tensor", "d_sync", dvv)
            for qt in range(QT):
                avp = PS[2] if qt % 2 == 0 else PS[5]
                gq = h * QT + qt
                avr = 2 if QT >= 2 else 1
                if gq - avr in av_free:
                    p.wait("tensor", "c_dv", av_free[gq - avr])
                def emit_av2(kc_a, gkc_a, last):
                    for hf in range(2):
                        ebp = gE[:, ((gkc_a % 2) * 2 + hf) * 256:
                                 ((gkc_a % 2) * 2 + hf) * 256 + 256]
                        p.wait("tensor", "c_sc", exp_done[(gkc_a, hf)])
                        TE(lambda e, vb=vb, avp=avp, kc_a=kc_a, ebp=ebp, hf=hf:
                           e.matmul(
                               avp[0:65, hf * 256:hf * 256 + 256],
                               vb[:, kc_a * 65:kc_a * 65 + 65], ebp,
                               start=(kc_a == 0 and hf == 0),
                               stop=(last and hf == 1)),
                           inc=("c_pe" if last and hf == 1 else None))

                for kc in range(TNK):
                    ps = PS[gkc % 2]
                    if (gkc - 2, 1) in exp_done:
                        p.wait("tensor", "c_sc", exp_done[(gkc - 2, 1)])
                    TE(lambda e, ps=ps, kfm=kfm, qfm=qfm, kc=kc, qt=qt: e.matmul(
                        ps[:, 0:512], kfm[0:64, kc * 128:(kc + 1) * 128],
                        qfm[0:64, qt * 512:(qt + 1) * 512], start=True, stop=True),
                       inc="c_pe")
                    qk_done = p.now("c_pe")
                    p.wait("scalar", "c_pe", qk_done)
                    for hf in range(2):
                        eb = gE[:, ((gkc % 2) * 2 + hf) * 256:
                                ((gkc % 2) * 2 + hf) * 256 + 256]
                        exp_done[(gkc, hf)] = SC(
                            lambda e, ps=ps, eb=eb, hf=hf: e.activation(
                                out=eb, in_=ps[:, hf * 256:hf * 256 + 256],
                                func=AF.Exp, bias=csc[:, 4:5],
                                scale=0.125), inc="c_sc")
                    if kc >= 1:
                        emit_av2(kc - 1, gkc - 1, False)
                    gkc += 1
                emit_av2(TNK - 1, gkc - 1, True)
                p.wait("vector", "c_pe")
                V(lambda e, avp=avp, qt=qt: e.tensor_copy(
                    sm_r[0:1, qt * 512:(qt + 1) * 512], avp[64:65, 0:512]),
                  inc="c_dv")
                p.wait("vector", "c_dv")
                V(lambda e, qt=qt: e.reciprocal(sm_r[0:1, qt * 512:(qt + 1) * 512],
                                                sm_r[0:1, qt * 512:(qt + 1) * 512]),
                  inc="c_dv", settle=True)
                p.wait("gpsimd", "c_dv")
                p.dma("gpsimd", s_bc[0:512], sm_r[0:1, qt * 512:(qt + 1) * 512])
                bc_dv = p.dma("gpsimd", gE[0:64, 1024:1536],
                              s_bc[None, 0:512].broadcast_to((64, 512)))
                p.wait("vector", "d_gpsimd", bc_dv)
                av_free[gq] = V(TT(gE[0:64, 1536:2048], avp[0:64, 0:512],
                                   gE[0:64, 1024:1536], ALU.mult), inc="c_dv",
                                settle=False)
                p.wait("gpsimd", "c_dv")
                p.dma("gpsimd", s_ao[h * 64:(h + 1) * 64, qt * 512:(qt + 1) * 512],
                      gE[0:64, 1536:2048])
            head_done[h] = p.now("c_pe")
        ao_done = p.now("d_gpsimd")

        pw_bufs = [gA, gB, gC, gD]
        p.wait("sync", "c_pe")
        p.wait("sync", "c_sc")
        for dc in range(DC):
            buf = pw_bufs[dc // 2]
            p.dma("sync", buf[:, (dc % 2) * 1024:(dc % 2) * 1024 + D],
                  wn["pw"].rearrange("(c p) d -> c p d", p=128)[dc])
        pw_dv = p.now("d_sync")
        proj_done = {}
        prj_bufs = (gH, gK)
        for g in range(TN):
            if g >= 2:
                p.wait("scalar", "c_pe", proj_done[g - 2])
            if g == 0:
                p.wait("scalar", "d_gpsimd", ao_done)
                p.wait("scalar", "c_pe")
            dvh = p.dma("scalar",
                        prj_bufs[g % 2][:, 0:DC * 128].rearrange(
                            "p (c u) -> p c u", c=DC),
                        s_ao.rearrange("(c p) s -> p c s", p=128)[:, :,
                                                                  g * 128:(g + 1) * 128])
            p.wait("tensor", "d_scalar", dvh)
            if g == 0:
                p.wait("tensor", "d_sync", pw_dv)
            NW = min(D, 512)
            for n in range(max(1, D // 512)):
                ps = PS[4 + n % 2]
                for dc in range(DC):
                    buf = pw_bufs[dc // 2]
                    pb_ = prj_bufs[g % 2]
                    TE(lambda e, ps=ps, buf=buf, pb_=pb_, dc=dc, n=n, NW=NW:
                       e.matmul(
                        ps[:, 0:NW], pb_[:, dc * 128:(dc + 1) * 128],
                        buf[:, (dc % 2) * 1024 + n * 512:
                            (dc % 2) * 1024 + n * 512 + NW],
                        start=(dc == 0), stop=(dc == DC - 1)),
                       inc=("c_pe" if dc == DC - 1 else None))
                p.wait("vector", "c_pe")
                xsl = x_tile(g)[:, n * 512:n * 512 + NW]
                V(TT(gM[:, 0:NW], ps[:, 0:NW], pbbc[:, n * 512:n * 512 + NW],
                     ALU.add), settle=False)
                V(TT(xsl, gM[:, 0:NW], xsl, ALU.add), inc="c_dv", settle=False)
                p.wait("tensor", "c_dv")
            proj_done[g] = p.now("c_pe")

    for seq in range(NSEQ):
        in_ap = x_h[seq].rearrange("(t p) d -> p t d", p=128)
        p.wait("gpsimd", "c_dv")
        xdv = p.dma("gpsimd", x_res[:].rearrange("p (t d) -> p t d", t=TN), in_ap)
        p.wait("vector", "d_gpsimd", xdv)
        p.wait("scalar", "d_gpsimd", xdv)
        if seq == 0:
            p.wait("scalar", "d_scalar", consts_dv)
            p.wait("vector", "d_scalar", consts_dv)
        for ph in phases:
            if ph == 'f':
                ffn(seq)
            else:
                attn(seq)
        p.wait("gpsimd", "c_dv")
        out_ap = o_h[seq].rearrange("(t p) d -> p t d", p=128)
        p.dma("gpsimd", out_ap, x_res[:].rearrange("p (t d) -> p t d", t=TN))

    for s in ("d_gpsimd", "d_sync", "d_scalar"):
        p.wait("sync", s)

    with nc.Block() as block:
        p.build(block)
    return nc, es


_CACHED = {}


def get_nc(NSEQ, S, D, HID, HEADS, phases='faf'):
    key = (NSEQ, S, D, HID, HEADS, phases)
    if key not in _CACHED:
        _CACHED[key] = build_kernel(NSEQ, S, D, HID, HEADS, phases)
    return _CACHED[key][0]


def kernel(L0_attn_x, L0_attn_q_w1, L0_attn_q_b1, L0_attn_q_w2, L0_attn_q_b2,
           L0_attn_k_w1, L0_attn_k_b1, L0_attn_k_w2, L0_attn_k_b2,
           L0_attn_v_w1, L0_attn_v_b1, L0_attn_v_w2, L0_attn_v_b2,
           L0_attn_proj_w, L0_attn_proj_b,
           L0_ffn_w1, L0_ffn_conv_w, L0_ffn_conv_b, L0_ffn_w2):
    x = np.ascontiguousarray(np.asarray(L0_attn_x, dtype=np.float32))
    B, S, D = x.shape
    HID = np.asarray(L0_ffn_w1).shape[0]
    HEADS = D // 64
    NSEQ = B // N_CORES
    nc = get_nc(NSEQ, S, D, HID, HEADS)
    w = dict(
        qw1=L0_attn_q_w1, qb1=L0_attn_q_b1, qw2=L0_attn_q_w2, qb2=L0_attn_q_b2,
        kw1=L0_attn_k_w1, kb1=L0_attn_k_b1, kw2=L0_attn_k_w2, kb2=L0_attn_k_b2,
        vw1=L0_attn_v_w1, vb1=L0_attn_v_b1, vw2=L0_attn_v_w2, vb2=L0_attn_v_b2,
        pw=L0_attn_proj_w, pb=L0_attn_proj_b,
        fw1=L0_ffn_w1, cw=L0_ffn_conv_w, cb=L0_ffn_conv_b, fw2=L0_ffn_w2)
    w = {k: np.ascontiguousarray(np.asarray(v, dtype=np.float32)) for k, v in w.items()}
    in_maps = [dict(w, x=x[i * NSEQ:(i + 1) * NSEQ]) for i in range(N_CORES)]
    res = run_bass_kernel_spmd(nc, in_maps, core_ids=list(range(N_CORES)))
    out = np.concatenate([r["out"] for r in res.results], axis=0)
    return out.astype(np.float32)


# version 19
# speedup vs baseline: 1.0916x; 1.0916x over previous
from contextlib import ExitStack

import numpy as np

import concourse.bass as bass
import concourse.mybir as mybir
from concourse.bass_utils import run_bass_kernel_spmd

F32 = mybir.dt.float32
BF16 = mybir.dt.bfloat16
I32 = mybir.dt.int32
AF = mybir.ActivationFunctionType
ALU = mybir.AluOpType
AX = mybir.AxisListType.X

MAGIC = 12582912.0
LN_EPS = 1e-5
QCLIP = 1e-5
N_CORES = 8


class Prog:
    ENGS = ("sync", "scalar", "vector", "tensor", "gpsimd")

    def __init__(self, nc, es):
        self.nc = nc
        self.es = es
        self.streams = {k: [] for k in self.ENGS}
        self.sem_n = {}
        self._sems = {}

    def sem(self, name):
        if name not in self._sems:
            self._sems[name] = self.es.enter_context(self.nc.semaphore(name))
            self.sem_n[name] = 0
        return self._sems[name]

    def wait(self, eng, name, val=None):
        h = self.sem(name)
        v = self.sem_n[name] if val is None else val
        if v <= 0:
            return
        self.streams[eng].append(lambda e, h=h, v=v: e.wait_ge(h, v))

    def now(self, name):
        self.sem(name)
        return self.sem_n[name]

    _MILE = {"vector": "c_dv", "scalar": "c_sc", "tensor": "c_pe",
             "gpsimd": "c_gp", "sync": "c_sy"}

    def do(self, eng, fn, inc=None, k=1, settle=False):
        if settle and inc is None:
            inc = self._MILE[eng]
        if inc is not None:
            h = self.sem(inc)
            self.sem_n[inc] += k
            self.streams[eng].append(lambda e, fn=fn, h=h, k=k: fn(e).then_inc(h, k))
        else:
            self.streams[eng].append(lambda e, fn=fn: fn(e))
        if settle:
            self.wait(eng, inc)
        return self.sem_n.get(inc)

    def dma(self, eng, out, in_, transpose=False, slow=False, nowait=False):
        if not nowait:
            self.wait(eng, "d_" + eng)
        return self.do(
            eng,
            lambda e, o=out, i=in_, t=transpose, s=slow: e.dma_start(
                out=o, in_=i, transpose=t, allow_slow_non_contiguous=s),
            inc="d_" + eng, k=16)

    def build(self, block):
        for eng in self.ENGS:
            fns = self.streams[eng]
            if not fns:
                continue

            def body(e, fns=fns):
                for f in fns:
                    f(e)
            getattr(block, eng)(body)


def build_kernel(NSEQ, S, D, HID, HEADS, phases='faf'):
    DH = D // HEADS
    DC = D // 128
    HC = HID // 128
    TN = S // 128
    QT = S // 512
    NQ = S // 512
    SB_W = max(S, 2048)
    assert DH == 64 and D % 128 == 0 and HID % 128 == 0 and S % 512 == 0

    nc = bass.Bass("TRN2", target_bir_lowering=False,
                   detect_race_conditions=False)
    es = ExitStack()
    es.__enter__()
    p = Prog(nc, es)

    x_h = nc.dram_tensor("x", [NSEQ, S, D], F32, kind="ExternalInput").ap()
    o_h = nc.dram_tensor("out", [NSEQ, S, D], F32, kind="ExternalOutput").ap()
    wn = {}
    for nm, shp in [("qw1", [D, HID]), ("qb1", [HID]), ("qw2", [HID, D]), ("qb2", [D]),
                    ("kw1", [D, HID]), ("kb1", [HID]), ("kw2", [HID, D]), ("kb2", [D]),
                    ("vw1", [D, HID]), ("vb1", [HID]), ("vw2", [HID, D]), ("vb2", [D]),
                    ("pw", [D, D]), ("pb", [D]),
                    ("fw1", [HID, D]), ("cw", [HID, 1, 3]), ("cb", [HID]),
                    ("fw2", [D, HID])]:
        wn[nm] = nc.dram_tensor(nm, shp, F32, kind="ExternalInput").ap()

    import os as _os
    _K = "ExternalOutput" if _os.environ.get("KERNEL_DEBUG") else "Internal"
    s_w1qT = nc.dram_tensor("s_w1qT", [HID // 128, 128, D], BF16,
                            kind=_K).ap()
    NDH = max(1, D // 512)
    DHALF = D // NDH
    s_w2t4 = nc.dram_tensor("s_w2t4", [NDH, 128, HID * DHALF // 128], BF16,
                            kind=_K).ap()
    s_sil = nc.dram_tensor("s_sil", [HC, 128, S], F32, kind=_K).ap()
    s_xq2 = nc.dram_tensor("s_xq2", [HC, 128, S], BF16, kind=_K).ap()
    s_yT = nc.dram_tensor("s_yT", [D, S], F32, kind=_K).ap()
    s_q = nc.dram_tensor("s_q", [D, S], F32, kind=_K).ap()
    s_k = nc.dram_tensor("s_k", [D, S], F32, kind=_K).ap()
    s_v = nc.dram_tensor("s_v", [S, D], F32, kind=_K).ap()
    s_ao = nc.dram_tensor("s_ao", [D, S], F32, kind=_K).ap()
    s_bc = nc.dram_tensor("s_bc", [4096], F32, kind="Internal").ap()
    s_bc2 = nc.dram_tensor("s_bc2", [64], F32, kind="Internal").ap()
    s_w1t3 = nc.dram_tensor("s_w1t3", [3, HID // 128, 128, D], F32,
                            kind="Internal").ap()

    def sb(name, shape, dt):
        return es.enter_context(nc.sbuf_tensor(name, shape, dt))

    x_res = sb("x_res", [128, TN * D], F32)
    sh16k = sb("sh16k", [128, DC * S], BF16)
    gA = sb("gA", [128, SB_W], F32)
    gB = sb("gB", [128, SB_W], F32)
    gC = sb("gC", [128, SB_W], F32)
    gD = sb("gD", [128, SB_W], F32)
    gE = sb("gE", [128, SB_W], F32)
    gF = sb("gF", [128, SB_W], F32)
    gH = sb("gH", [128, 1024], F32)
    gI = sb("gI", [128, 1024], F32)
    gK = sb("gK", [128, 1024], F32)
    gL = sb("gL", [128, 1024], F32)
    gM = sb("gM", [128, 1024], F32)
    gbfA = sb("gbfA", [128, 2048], BF16)
    gXQa = sb("gXQa", [128, 4096], BF16)
    gXQb = sb("gXQb", [128, 4096], BF16)
    w1ta = sb("w1ta", [128, 1024], BF16)
    w1tb = sb("w1tb", [128, 1024], BF16)
    stat2 = sb("stat2", [128, 13 * 16], F32)
    stat3 = sb("stat3", [128, 64], F32)
    cwc = sb("cwc", [128, 4 * HC], F32)
    b1col = sb("b1col", [128, 3 * HC], F32)
    b2col = sb("b2col", [128, 3 * DC], F32)
    pbbc = sb("pbbc", [128, 1024], F32)
    b2bc = sb("b2bc", [128, 1024], F32)
    ones_r = sb("ones_r", [1, 128], F32)
    sm_r = sb("sm_r", [1, 2048], F32)
    csc = sb("csc", [128, 8], F32)
    identb = sb("identb", [128, 128], BF16)
    identf = sb("identf", [128, 128], F32)

    PS = [es.enter_context(nc.psum_tensor(f"ps{i}", [128, 512], F32)) for i in range(6)]
    PT = es.enter_context(nc.psum_tensor("pt", [128, 1024], BF16))
    P7 = es.enter_context(nc.psum_tensor("p7", [128, 512], F32))

    SLOT = {"msum": 0, "mx": 1, "mn": 2, "ssq": 3, "m": 4, "var": 5, "rstd": 6,
            "amaxc": 7, "sprime": 8, "biasra": 9, "inva": 10, "s2tok": 11,
            "lnb": 12}

    def st(name, i):
        b = SLOT[name]
        return stat2[:, b * 16 + i: b * 16 + i + 1]

    def stb(name, n=None):
        b = SLOT[name]
        n = TN if n is None else n
        return stat2[:, b * 16: b * 16 + n]

    def x_tile(t):
        return x_res[:, t * D:(t + 1) * D]

    V = lambda fn, inc=None, settle=True: p.do("vector", fn, inc=inc, settle=settle)
    SC = lambda fn, inc=None, settle=False: p.do("scalar", fn, inc=inc, settle=settle)
    TE = lambda fn, inc=None: p.do("tensor", fn, inc=inc)
    GP = lambda fn, inc=None: p.do("gpsimd", fn, inc=inc)

    def TS1(out, in0, scalar, op):
        return lambda e, o=out, i=in0, s=scalar, op=op: e.tensor_single_scalar(
            out=o, in_=i, scalar=s, op=op)

    def TS2(out, in0, s1, s2, op0, op1):
        return lambda e, o=out, i=in0, a=s1, b=s2, op0=op0, op1=op1: e.tensor_scalar(
            out=o, in0=i, scalar1=a, scalar2=b, op0=op0, op1=op1)

    def TT(out, in0, in1, op):
        return lambda e, o=out, a=in0, b=in1, op=op: e.tensor_tensor(
            out=o, in0=a, in1=b, op=op)

    GP(lambda e: e.memset(identb[:], 0.0), inc="c_gp")
    GP(lambda e: e.memset(identf[:], 0.0), inc="c_gp")
    GP(lambda e: e.memset(ones_r[:], 1.0), inc="c_gp")
    GP(lambda e: e.memset(csc[:, 5:6], MAGIC), inc="c_gp")
    GP(lambda e: e.memset(csc[:, 6:7], LN_EPS), inc="c_gp")
    p.wait("gpsimd", "c_gp")
    GP(lambda e: e.affine_select(out=identb[:], in_=identb[:],
                                 compare_op=ALU.not_equal, fill=1.0, base=0,
                                 pattern=[[-1, 128]], channel_multiplier=1),
       inc="c_gp")
    GP(lambda e: e.affine_select(out=identf[:], in_=identf[:],
                                 compare_op=ALU.not_equal, fill=1.0, base=0,
                                 pattern=[[-1, 128]], channel_multiplier=1),
       inc="c_gp")
    p.wait("tensor", "c_gp")
    p.wait("scalar", "c_gp")
    p.wait("vector", "c_gp")

    cw_r = wn["cw"].rearrange("(c p) one t -> p (one t) c", p=128)
    for t in range(3):
        p.dma("scalar", cwc[:, t * HC:(t + 1) * HC], cw_r[:, t, :], slow=True)
    p.dma("scalar", cwc[:, 3 * HC:4 * HC],
          wn["cb"].rearrange("(c p) -> p c", p=128), slow=True)
    for i, nm in enumerate(("qb1", "kb1", "vb1")):
        p.dma("scalar", b1col[:, i * HC:(i + 1) * HC],
              wn[nm].rearrange("(c p) -> p c", p=128), slow=True)
    for i, nm in enumerate(("qb2", "kb2", "vb2")):
        p.dma("scalar", b2col[:, i * DC:(i + 1) * DC],
              wn[nm].rearrange("(c p) -> p c", p=128), slow=True)
    p.dma("scalar", pbbc[:, 0:D], wn["pb"][None, :].broadcast_to((128, D)))
    p.dma("scalar", b2bc[:, 0:D], wn["vb2"][None, :].broadcast_to((128, D)))
    consts_dv = p.now("d_scalar")

    def quant_ffn_w(w_ap, n_rows, n_cols, s_out, invw_slot, sw_slot,
                    tiled_out=False):
        RT = n_rows // 128
        CW = min(n_cols, 2048)
        NCB = n_cols // CW
        w_r = w_ap.rearrange("(r p) c -> r p c", p=128)
        V(lambda e: e.memset(stat3[:, 1:2], 0.0), inc="c_dv")
        for r in range(RT):
            for cb in range(NCB):
                p.wait("sync", "c_dv")
                dv = p.dma("sync", gA[:, 0:CW], w_r[r][:, cb * CW:(cb + 1) * CW])
                p.wait("vector", "d_sync", dv)
                V(lambda e: e.tensor_reduce(out=stat3[:, 0:1], in_=gA[:, 0:CW],
                                            axis=AX, op=ALU.add,
                                            apply_absolute_value=True), settle=True)
                V(TT(stat3[:, 1:2], stat3[:, 1:2], stat3[:, 0:1], ALU.add), inc="c_dv")
        V(lambda e: e.memset(stat3[:, 2:3], 1.0), inc="c_dv")
        p.wait("tensor", "c_dv")
        TE(lambda e: e.matmul(PS[0][0:1, 0:1], stat3[:, 1:2], stat3[:, 2:3],
                              start=True, stop=True), inc="c_pe")
        p.wait("vector", "c_pe")
        V(TS2(stat3[0:1, 3:4], PS[0][0:1, 0:1], 1.0 / (n_rows * n_cols), QCLIP,
              ALU.mult, ALU.max))
        V(lambda e: e.reciprocal(stat3[0:1, 4:5], stat3[0:1, 3:4]), inc="c_dv",
          settle=True)
        p.wait("gpsimd", "c_dv")
        p.wait("gpsimd", "d_scalar")
        p.dma("gpsimd", s_bc2[0:2], stat3[0:1, 3:5])
        p.wait("scalar", "d_gpsimd")
        p.dma("scalar", csc[:, invw_slot:invw_slot + 1],
              s_bc2[0:1].broadcast_to((128, 1)))
        sw_ready = p.dma("scalar", csc[:, sw_slot:sw_slot + 1],
                         s_bc2[1:2].broadcast_to((128, 1)))
        p.wait("sync", "c_dv")
        for r in range(RT):
            for cb in range(NCB):
                p.wait("sync", "c_sc")
                dv = p.dma("sync", gA[:, 0:CW], w_r[r][:, cb * CW:(cb + 1) * CW])
                p.wait("scalar", "d_sync", dv)
                p.wait("scalar", "d_scalar", sw_ready)
                p.wait("scalar", "c_dv")
                SC(lambda e: e.activation(
                    out=gB[:, 0:CW], in_=gA[:, 0:CW], func=AF.Identity,
                    bias=csc[:, 5:6], scale=csc[:, sw_slot:sw_slot + 1]), inc="c_sc")
                p.wait("vector", "c_sc")
                V(TS1(gB[:, 0:CW].bitcast(I32), gB[:, 0:CW], MAGIC, ALU.subtract),
                  settle=False)
                V(TS2(gbfA[:, 0:CW], gB[:, 0:CW].bitcast(I32), 1.0, -1.0,
                      ALU.min, ALU.max), inc="c_dv", settle=False)
                p.wait("tensor", "c_dv")
                p.wait("vector", "d_gpsimd")
                CT = CW // 128
                for c in range(CT):
                    last = (c % 8 == 7 or c == CT - 1)
                    TE(lambda e, c=c: e.transpose(
                        PT[:, (c % 8) * 128:(c % 8) * 128 + 128],
                        gbfA[:, c * 128:(c + 1) * 128], identb[:]),
                       inc=("c_pe" if last else None))
                    if last:
                        n8 = (c % 8) + 1
                        p.wait("vector", "c_pe")
                        V(lambda e, c=c, n8=n8: e.tensor_copy(
                            gXQa[:, (c - n8 + 1) * 128:(c + 1) * 128],
                            PT[:, 0:n8 * 128]), inc="c_dv", settle=False)
                        p.wait("tensor", "c_dv")
                p.wait("gpsimd", "c_dv")
                if tiled_out:
                    p.dma("gpsimd", s_out[r], gXQa[:, 0:CT * 128])
                else:
                    off = (r * 128) % DHALF
                    out_ap = s_out[(r * 128) // DHALF].rearrange(
                        "p (c u) -> p c u", u=DHALF)[:, cb * CT:(cb + 1) * CT,
                                                     off:off + 128]
                    p.dma("gpsimd", out_ap,
                          gXQa[:, 0:CT * 128].rearrange("p (c u) -> p c u", u=128))

    quant_ffn_w(wn["fw1"], HID, D, s_w1qT, 0, 1, tiled_out=True)
    w1qT_dv = p.now("d_gpsimd")

    quant_ffn_w(wn["fw2"], D, HID, s_w2t4, 2, 3)
    w2qT_dv = p.now("d_gpsimd")

    for wi, w1n in enumerate(("qw1", "kw1", "vw1")):
        w1s = wn[w1n].rearrange("(c p) h -> p c h", p=128)
        for hc in range(HC):
            p.dma("scalar",
                  s_w1t3[wi, hc].rearrange("p (c u) -> p c u", u=128),
                  w1s[:, :, hc * 128:(hc + 1) * 128], slow=True, nowait=True)
    w1t3_dv = p.now("d_scalar")

    def ln_stats(quant):
        p.wait("scalar", "c_dv")
        for t in range(TN):
            xt = x_tile(t)
            V(lambda e, xt=xt, t=t: e.tensor_reduce(out=st("msum", t), in_=xt,
                                                    axis=AX, op=ALU.add),
              inc="c_dv", settle=False)
            if quant:
                V(lambda e, xt=xt, t=t: e.tensor_reduce(out=st("mx", t), in_=xt,
                                                        axis=AX, op=ALU.max),
                  inc="c_dv", settle=False)
                V(lambda e, xt=xt, t=t: e.tensor_reduce(out=st("mn", t), in_=xt,
                                                        axis=AX, op=ALU.min),
                  inc="c_dv", settle=False)
            SC(lambda e, xt=xt, t=t: e.activation(out=gH[:, 0:D], in_=xt,
                                                  func=AF.Square, bias=0.0, scale=1.0,
                                                  accum_out=st("ssq", t)), inc="c_sc")
        p.wait("vector", "c_dv")
        p.wait("vector", "c_sc")
        V(TS1(stb("m"), stb("msum"), 1.0 / D, ALU.mult))
        V(TT(stb("var"), stb("m"), stb("m"), ALU.mult))
        V(TS1(stb("ssq"), stb("ssq"), 1.0 / D, ALU.mult))
        V(TT(stb("var"), stb("ssq"), stb("var"), ALU.subtract), inc="c_dv")
        p.wait("scalar", "c_dv")
        SC(lambda e: e.activation(out=stb("rstd"), in_=stb("var"), func=AF.Sqrt,
                                  bias=csc[:, 6:7], scale=1.0), inc="c_sc")
        p.wait("vector", "c_sc")
        V(lambda e: e.reciprocal(stb("rstd"), stb("rstd")), inc="c_dv",
          settle=True)

    def bcast_block(src_blk, n, dst, width):
        p.wait("tensor", "c_dv")
        TE(lambda e: e.transpose(P7[0:n, 0:128], src_blk, identf[:]), inc="c_pe")
        p.wait("vector", "c_pe")
        p.wait("vector", "d_gpsimd")
        V(lambda e, n=n: e.tensor_copy(gH[0:n, 0:128], P7[0:n, 0:128]), inc="c_dv")
        p.wait("gpsimd", "c_dv")
        p.dma("gpsimd", s_bc[0:width], gH[0:n, 0:128])
        p.wait("gpsimd", "c_dv")
        return p.dma("gpsimd", dst, s_bc[None, 0:width].broadcast_to((128, width)))

    def ffn(seq):
        ln_stats(quant=True)
        V(TT(stb("mx"), stb("mx"), stb("m"), ALU.subtract))
        V(TT(stb("mn"), stb("m"), stb("mn"), ALU.subtract))
        V(TT(stb("mx"), stb("mx"), stb("mn"), ALU.max))
        V(TT(stb("amaxc"), stb("mx"), stb("rstd"), ALU.mult))
        V(TS1(stb("amaxc"), stb("amaxc"), QCLIP, ALU.max), inc="c_dv")
        p.wait("vector", "c_dv")
        V(lambda e: e.reciprocal(stb("sprime"), stb("amaxc")), settle=True)
        V(TS1(stb("sprime"), stb("sprime"), 127.0, ALU.mult))
        V(TT(stb("sprime"), stb("sprime"), stb("rstd"), ALU.mult))
        V(TT(stb("biasra"), stb("m"), stb("sprime"), ALU.mult))
        V(TS1(stb("biasra"), stb("biasra"), -1.0, ALU.mult))
        V(TS1(stb("inva"), stb("amaxc"), 1.0 / 127.0, ALU.mult))
        V(TS1(stb("inva"), stb("inva"), csc[:, 0:1], ALU.mult), inc="c_dv")

        for t in range(TN):
            xt = x_tile(t)
            p.wait("scalar", "c_dv")
            SC(lambda e, xt=xt, t=t: e.activation(out=gH[:, 0:D], in_=xt, func=AF.Identity,
                                                  bias=st("biasra", t),
                                                  scale=st("sprime", t)), inc="c_sc")
            p.wait("vector", "c_sc")
            V(TS2(gI[:, 0:D].bitcast(I32), gH[:, 0:D], MAGIC, MAGIC,
                  ALU.add, ALU.subtract), settle=False)
            V(lambda e: e.tensor_copy(gbfA[:, 0:D], gI[:, 0:D].bitcast(I32)),
              inc="c_dv", settle=False)
            p.wait("tensor", "c_dv")
            for dc in range(DC):
                TE(lambda e, dc=dc: e.transpose(PT[:, dc * 128:(dc + 1) * 128],
                                                gbfA[:, dc * 128:(dc + 1) * 128],
                                                identb[:]),
                   inc=("c_pe" if dc == DC - 1 else None))
            p.wait("vector", "c_pe")
            V(lambda e, t=t: e.tensor_copy(
                sh16k[:].rearrange("p (c s) -> p c s", c=DC)[:, :, t * 128:(t + 1) * 128],
                PT[:, 0:DC * 128].rearrange("p (c u) -> p c u", c=DC)), inc="c_dv",
              settle=False)
            p.wait("tensor", "c_dv")
        xqT_done = p.now("c_dv")

        inva_dv = bcast_block(stb("inva"), TN, gF[:, 0:S], S)

        p.wait("vector", "d_gpsimd", inva_dv)
        p.wait("vector", "c_gp")
        V(lambda e: e.memset(gE[:, 0:S], 0.0), inc="c_gp", settle=False)
        sil_bufs = (gC, gD)
        w1_bufs = (w1ta, w1tb)
        ga_ping = (gA, gXQa.bitcast(F32))
        w1r = s_w1qT
        pe_done = {}
        spill_done = {}
        evac_done = {}
        center_done = {}
        c_done = {}
        silu_done = {}
        abs_done = {}
        fold_done = {}
        p.wait("sync", "d_gpsimd", w1qT_dv)
        for hc in range(HC + 2):
            if hc < HC:
                wb = w1_bufs[hc % 2]
                ga = ga_ping[hc % 2]
                if hc >= 2:
                    p.wait("sync", "c_pe", pe_done[hc - 2])
                dv = p.dma("sync", wb[:, 0:DC * 128], w1r[hc])
                p.wait("tensor", "d_sync", dv)
                if hc == 0:
                    p.wait("tensor", "c_dv", xqT_done)
                if hc >= 2:
                    p.wait("vector", "c_sc", center_done[hc - 2])
                for q in range(NQ):
                    gi = hc * NQ + q
                    ps = PS[gi % 2]
                    if gi - 2 in evac_done:
                        p.wait("tensor", "c_dv", evac_done[gi - 2])
                    for dc in range(DC):
                        TE(lambda e, ps=ps, wb=wb, dc=dc, q=q: e.matmul(
                            ps[:, 0:512], wb[:, dc * 128:(dc + 1) * 128],
                            sh16k[:, dc * S + q * 512: dc * S + (q + 1) * 512],
                            start=(dc == 0), stop=(dc == DC - 1)),
                           inc=("c_pe" if dc == DC - 1 else None))
                    p.wait("vector", "c_pe")
                    evac_done[gi] = V(
                        TT(ga[:, q * 512:(q + 1) * 512], ps[:, 0:512],
                           gF[:, q * 512:(q + 1) * 512], ALU.mult), inc="c_dv",
                        settle=False)
                pe_done[hc] = p.now("c_pe")
            if 2 <= hc and hc - 2 < HC:
                f = hc - 2
                p.wait("vector", "c_sc", abs_done[f])
                fold_done[f] = V(TT(gE[:, 0:S], gE[:, 0:S], gB[:, 0:S], ALU.max),
                                 inc="c_gp", settle=False)
            if 1 <= hc and hc - 1 < HC:
                c = hc - 1
                ga = ga_ping[c % 2]
                p.wait("scalar", "c_dv", evac_done[c * NQ + NQ - 1])
                if c >= 1:
                    p.wait("scalar", "c_gp", fold_done[c - 1])
                SC(lambda e, c=c, ga=ga: e.activation(
                    out=gB[:, 0:S], in_=ga[:, 0:S], func=AF.Identity,
                    bias=cwc[:, 3 * HC + c:3 * HC + c + 1],
                    scale=cwc[:, HC + c:HC + c + 1]), inc="c_sc")
                center_done[c] = p.now("c_sc")
                p.wait("vector", "c_sc", center_done[c])
                V(lambda e, c=c, ga=ga: e.scalar_tensor_tensor(
                    out=gB[:, 1:S], in0=ga[:, 0:S - 1], scalar=cwc[:, c:c + 1],
                    in1=gB[:, 1:S], op0=ALU.mult, op1=ALU.add), settle=False)
                c_done[c] = V(lambda e, c=c, ga=ga: e.scalar_tensor_tensor(
                    out=gB[:, 0:S - 1], in0=ga[:, 1:S],
                    scalar=cwc[:, 2 * HC + c:2 * HC + c + 1],
                    in1=gB[:, 0:S - 1], op0=ALU.mult, op1=ALU.add), inc="c_dv",
                    settle=False)
                sil = sil_bufs[c % 2]
                if c >= 2:
                    p.wait("scalar", "d_gpsimd", spill_done[c - 2])
                p.wait("scalar", "c_dv", c_done[c])
                silu_done[c] = SC(lambda e, sil=sil: e.activation(
                    out=sil[:, 0:S], in_=gB[:, 0:S], func=AF.Silu,
                    bias=0.0, scale=1.0), inc="c_sc")
                abs_done[c] = SC(lambda e, sil=sil: e.activation(
                    out=gB[:, 0:S], in_=sil[:, 0:S], func=AF.Abs,
                    bias=0.0, scale=1.0), inc="c_sc")
                p.wait("gpsimd", "c_sc", silu_done[c])
                spill_done[c] = p.dma("gpsimd", s_sil[c], sil[:, 0:S])

        p.wait("tensor", "c_dv")
        p.wait("tensor", "c_gp")
        for i in range(TN):
            TE(lambda e, i=i: e.transpose(PS[i % 4][:, 0:128],
                                          gE[:, i * 128:(i + 1) * 128], identf[:]),
               inc="c_pe")
            p.wait("vector", "c_pe")
            V(lambda e, i=i: e.tensor_reduce(out=st("mx", i), in_=PS[i % 4][:, 0:128],
                                             axis=AX, op=ALU.max), inc="c_dv",
              settle=False)
            if i >= 3:
                p.wait("tensor", "c_dv", p.now("c_dv") - 3)
        p.wait("vector", "c_dv")
        V(TS1(stb("mx"), stb("mx"), QCLIP, ALU.max), inc="c_dv")
        p.wait("vector", "c_dv")
        V(lambda e: e.reciprocal(stb("amaxc"), stb("mx")), settle=True)
        V(TS1(stb("sprime"), stb("amaxc"), 127.0, ALU.mult))
        V(TS2(stb("s2tok"), stb("mx"), 1.0 / 127.0, csc[:, 2:3], ALU.mult, ALU.mult))
        V(TS1(stb("s2tok"), stb("s2tok"), 0.5, ALU.mult), inc="c_dv")
        s2_dv = bcast_block(stb("sprime"), TN, gF[:, 0:S], S)

        p.wait("vector", "d_gpsimd", s2_dv)
        p.wait("sync", "d_gpsimd", spill_done[HC - 1])
        p.wait("sync", "c_sc")
        xb_ping = (gI.bitcast(BF16), gM.bitcast(BF16))
        use_done = {}
        xq2_spill = {}
        for hc in range(HC):
            buf = sil_bufs[hc % 2]
            xbuf = xb_ping[hc % 2]
            if hc >= 2:
                p.wait("sync", "c_dv", use_done[hc - 2])
            dv = p.dma("sync", buf[:, 0:S], s_sil[hc])
            p.wait("vector", "d_sync", dv)
            if hc >= 2:
                p.wait("vector", "d_gpsimd", xq2_spill[hc - 2])
            V(TT(gB[:, 0:S], buf[:, 0:S], gF[:, 0:S], ALU.mult), settle=False)
            use_done[hc] = p.now("c_dv") + 1
            V(TS2(gA[:, 0:S].bitcast(I32), gB[:, 0:S], MAGIC, MAGIC,
                  ALU.add, ALU.subtract), settle=False)
            V(lambda e, xbuf=xbuf: e.tensor_copy(xbuf[:, 0:S],
                                                 gA[:, 0:S].bitcast(I32)),
              inc="c_dv", settle=False)
            p.wait("gpsimd", "c_dv")
            xq2_spill[hc] = p.dma("gpsimd", s_xq2[hc], xbuf[:, 0:S])

        poolA = ((gXQa, 32), (gbfA, 16), (gH.bitcast(BF16), 16))
        poolB = ((gXQb, 32), (gK.bitcast(BF16), 16), (gL.bitcast(BF16), 16))

        def xq_tile(pool, t):
            for buf, cap in pool:
                if t < cap:
                    return buf[:, t * 128:(t + 1) * 128]
                t -= cap
            raise AssertionError

        def xq_range(pool, t0, t1):
            for buf, cap in pool:
                if t0 < cap:
                    assert t1 <= cap
                    return buf[:, t0 * 128:t1 * 128]
                t0 -= cap
                t1 -= cap
            raise AssertionError

        chunks = []
        base_hc = 0
        for _, cap in poolA:
            span = cap // 2
            lo = 0
            while lo < span and base_hc + lo < HC:
                hi = min(lo + 6, span, HC - base_hc)
                chunks.append((base_hc + lo, base_hc + hi))
                lo = hi
            base_hc += span
        qassign = {q: [] for q in ("sync", "scalar", "gpsimd")}
        for lo, hi in sorted(chunks, key=lambda c: c[0] - c[1]):
            q = min(qassign, key=lambda k: sum(b - a for a, b in qassign[k]))
            qassign[q].append((lo, hi))

        sx = s_xq2.rearrange("c p s -> p c s")
        p.wait("sync", "d_gpsimd", w2qT_dv)
        mm_done = {}
        stt_done = {}
        NG2 = (TN + 1) // 2
        for half in range(NDH):
            p.wait("sync", "c_pe")
            dv = p.dma("sync", sh16k[:, 0:HC * DHALF], s_w2t4[half])
            p.wait("tensor", "d_sync", dv)
            for g2 in range(NG2):
                pool = (poolA, poolB)[g2 % 2]
                gs = [g for g in (g2 * 2, g2 * 2 + 1) if g < TN]
                nu = len(gs) * 128
                guard = mm_done.get(half * NG2 + g2 - 2)
                dvqs = {}
                hc_dv = {}
                for qeng, rr in qassign.items():
                    if guard is not None:
                        p.wait(qeng, "c_pe", guard)
                    for lo, hi in rr:
                        p.wait(qeng, "d_gpsimd", xq2_spill[hi - 1])
                        dvqs[qeng] = p.dma(
                            qeng,
                            xq_range(pool, lo * 2, hi * 2).rearrange(
                                "p (c u) -> p c u", u=nu),
                            sx[:, lo:hi, g2 * nu:(g2 + 1) * nu])
                        for h_ in range(lo, hi):
                            hc_dv[h_] = (qeng, dvqs[qeng])
                waited = set()
                for g in gs:
                    sub = g % 2
                    NW = min(DHALF, 512)
                    for n in range(max(1, DHALF // 512)):
                        psi = 2 + 2 * (g % 2) + (n % 2)
                        ps = PS[psi]
                        if psi in stt_done:
                            p.wait("tensor", "c_dv", stt_done[psi])
                        for hcc in range(HC):
                            qeng, dvq = hc_dv[hcc]
                            if (qeng, dvq) not in waited:
                                p.wait("tensor", "d_" + qeng, dvq)
                                waited.add((qeng, dvq))
                            xt_ = xq_tile(pool, hcc * 2 + sub)
                            TE(lambda e, ps=ps, xt_=xt_, hcc=hcc, n=n, NW=NW:
                               e.matmul(
                                   ps[:, 0:NW], xt_,
                                   sh16k[:, hcc * DHALF + n * 512:
                                         hcc * DHALF + n * 512 + NW],
                                   start=(hcc == 0), stop=(hcc == HC - 1)),
                               inc=("c_pe" if hcc == HC - 1 else None))
                        p.wait("vector", "c_pe")
                        xsl = x_tile(g)[:, half * DHALF + n * 512:
                                        half * DHALF + n * 512 + NW]
                        stt_done[psi] = V(lambda e, ps=ps, xsl=xsl, g=g, NW=NW:
                          e.scalar_tensor_tensor(
                              out=xsl, in0=ps[:, 0:NW], scalar=st("s2tok", g),
                              in1=xsl, op0=ALU.mult, op1=ALU.add), inc="c_dv",
                          settle=False)
                mm_done[half * NG2 + g2] = p.now("c_pe")

    def attn(seq):
        ln_stats(quant=False)
        V(TT(stb("lnb"), stb("m"), stb("rstd"), ALU.mult))
        V(TS1(stb("lnb"), stb("lnb"), -1.0, ALU.mult), inc="c_dv")
        y_spill = 0
        for t in range(TN):
            xt = x_tile(t)
            p.wait("scalar", "c_dv")
            p.wait("scalar", "c_pe")
            SC(lambda e, xt=xt, t=t: e.activation(out=gH[:, 0:D], in_=xt, func=AF.Identity,
                                                  bias=st("lnb", t),
                                                  scale=st("rstd", t)), inc="c_sc")
            p.wait("tensor", "c_sc")
            for dc in range(DC):
                TE(lambda e, dc=dc: e.transpose(
                    PS[dc // 4][:, (dc % 4) * 128:(dc % 4) * 128 + 128],
                    gH[:, dc * 128:(dc + 1) * 128], identf[:]),
                   inc=("c_pe" if dc == DC - 1 else None))
            p.wait("vector", "c_pe")
            if t > 0:
                p.wait("vector", "d_gpsimd", y_spill)
            else:
                p.wait("vector", "d_gpsimd")
            for b in range((DC + 3) // 4):
                w = min(4, DC - b * 4) * 128
                V(lambda e, b=b, w=w: e.tensor_copy(gI[:, b * 512:b * 512 + w],
                                                    PS[b][:, 0:w]), inc="c_dv",
                  settle=False)
            p.wait("tensor", "c_dv")
            p.wait("gpsimd", "c_dv")
            out_ap = s_yT.rearrange("(c p) s -> p c s", p=128)[:, :, t * 128:(t + 1) * 128]
            y_spill = p.dma("gpsimd", out_ap,
                            gI[:, 0:DC * 128].rearrange("p (c u) -> p c u", c=DC))
        yT_dv = p.now("d_gpsimd")

        TT_ = 256
        NT = S // TT_
        w_bufs = ((gH, gK), (gI, gL), (gC, gD))
        hs_off = (0, 512, 1024)
        yb_bufs = (gF, gE)
        mm_hist = {}
        gelu_done = {}
        w2_dv = {}
        evac_sc = {}
        evac_dv = {}
        ghc = 0
        for pi, (w1n, w2n, s_dst, fm) in enumerate(
                (("qw1", "qw2", s_q, True), ("kw1", "kw2", s_k, True),
                 ("vw1", "vw2", s_v, False))):
            w1_ap = s_w1t3[pi]
            w2_ap = wn[w2n].rearrange("(c p) d -> c p d", p=128)

            def emit_mm2(j, fm=fm):
                ti_j, hcl = j // HC, j % HC
                wb2 = w_bufs[j % 3][1]
                hbuf = gA[:, hs_off[j % 3]:hs_off[j % 3] + TT_]
                p.wait("tensor", "c_sc", gelu_done[j])
                p.wait("tensor", "d_scalar", w2_dv[j])
                if hcl == 0 and ti_j >= 1:
                    if ti_j - 1 in evac_sc:
                        p.wait("tensor", "c_sc", evac_sc[ti_j - 1])
                    else:
                        p.wait("tensor", "c_dv", evac_dv[ti_j - 1])
                if fm:
                    for dc in range(DC):
                        psq = PS[2 + dc // 2]
                        off = (dc % 2) * TT_
                        TE(lambda e, psq=psq, off=off, wb2=wb2, dc=dc, hbuf=hbuf,
                           hcl=hcl: e.matmul(
                               psq[:, off:off + TT_],
                               wb2[:, dc * 128:(dc + 1) * 128], hbuf,
                               start=(hcl == 0 and dc % 2 == 0),
                               stop=(hcl == HC - 1 and dc % 2 == 1)),
                           inc=("c_pe" if dc == DC - 1 else None))
                else:
                    nd = max(1, D // 512)
                    NW = min(D, 512)
                    for g in range(TT_ // 128):
                        for n in range(nd):
                            psq = PS[2 + g * nd + n]
                            TE(lambda e, psq=psq, NW=NW, hbuf=hbuf, g=g, wb2=wb2,
                               n=n, hcl=hcl: e.matmul(
                                   psq[:, 0:NW], hbuf[:, g * 128:(g + 1) * 128],
                                   wb2[:, n * 512:n * 512 + NW],
                                   start=(hcl == 0), stop=(hcl == HC - 1)),
                               inc=("c_pe" if (g == TT_ // 128 - 1 and
                                               n == nd - 1) else None))
                mm_hist[j] = p.now("c_pe")

            for tt in range(NT):
                ti = pi * NT + tt
                yb = yb_bufs[ti % 2]
                if ti <= 1:
                    p.wait("sync", "c_pe")
                    if ti == 0:
                        p.wait("sync", "c_dv")
                        p.wait("sync", "d_gpsimd", yT_dv)
                        p.wait("sync", "d_scalar", w1t3_dv)
                else:
                    p.wait("sync", "c_pe", mm_hist[(ti - 2) * HC + HC - 1])
                p.dma("sync", yb[:, 0:DC * TT_].rearrange("p (c u) -> p c u", c=DC),
                      s_yT.rearrange("(c p) s -> p c s", p=128)[:, :,
                                                               tt * TT_:(tt + 1) * TT_])
                for hc in range(HC):
                    wb1, wb2 = w_bufs[ghc % 3]
                    if ghc >= 3:
                        p.wait("sync", "c_pe", mm_hist[ghc - 3])
                        p.wait("scalar", "c_pe", mm_hist[ghc - 3])
                    dv1 = p.dma("sync", wb1[:, 0:DC * 128], w1_ap[hc])
                    w2_dv[ghc] = p.dma("scalar", wb2[:, 0:D], w2_ap[hc])
                    p.wait("tensor", "d_sync", dv1)
                    ps1 = PS[ghc % 2]
                    if ghc >= 2:
                        p.wait("tensor", "c_sc", gelu_done[ghc - 2])
                    for dc in range(DC):
                        TE(lambda e, ps1=ps1, wb1=wb1, dc=dc, yb=yb: e.matmul(
                            ps1[:, 0:TT_], wb1[:, dc * 128:(dc + 1) * 128],
                            yb[:, dc * TT_:(dc + 1) * TT_],
                            start=(dc == 0), stop=(dc == DC - 1)),
                           inc=("c_pe" if dc == DC - 1 else None))
                    p.wait("scalar", "c_pe")
                    hbuf = gA[:, hs_off[ghc % 3]:hs_off[ghc % 3] + TT_]
                    gelu_done[ghc] = SC(
                        lambda e, ps1=ps1, hbuf=hbuf, pi=pi, hc=hc: e.activation(
                            out=hbuf, in_=ps1[:, 0:TT_], func=AF.Gelu,
                            bias=b1col[:, pi * HC + hc:pi * HC + hc + 1], scale=1.0),
                        inc="c_sc")
                    if hc >= 1:
                        emit_mm2(ghc - 1)
                    ghc += 1
                emit_mm2(ghc - 1)
                p.wait("scalar", "c_pe")
                if fm:
                    p.wait("scalar", "d_gpsimd")
                    p.wait("scalar", "c_dv")
                    for dc in range(DC):
                        psq = PS[2 + dc // 2]
                        off = (dc % 2) * TT_
                        SC(lambda e, psq=psq, off=off, dc=dc, pi=pi, tt=tt: e.activation(
                            out=gB[:, dc * TT_:(dc + 1) * TT_], in_=psq[:, off:off + TT_],
                            func=AF.Identity, bias=b2col[:, pi * DC + dc:pi * DC + dc + 1],
                            scale=1.0), inc="c_sc")
                    p.wait("vector", "c_sc")
                    V(lambda e, tt=tt: e.tensor_reduce(
                        out=stat3[:, 32 + tt:33 + tt], in_=gB[:, 0:DC * TT_], axis=AX,
                        op=ALU.max, apply_absolute_value=True), inc="c_dv",
                      settle=True)
                    p.wait("gpsimd", "c_dv")
                    out_ap = s_dst.rearrange("(c p) s -> p c s", p=128)[:, :,
                                                                        tt * TT_:(tt + 1) * TT_]
                    p.dma("gpsimd", out_ap,
                          gB[:, 0:DC * TT_].rearrange("p (c u) -> p c u", c=DC))
                    evac_sc[ti] = p.now("c_sc")
                else:
                    nd = max(1, D // 512)
                    NW = min(D, 512)
                    p.wait("vector", "c_pe")
                    for g in range(TT_ // 128):
                        p.wait("vector", "d_gpsimd")
                        for n in range(nd):
                            psq = PS[2 + g * nd + n]
                            V(TT(gM[:, n * 512:n * 512 + NW], psq[:, 0:NW],
                                 b2bc[:, n * 512:n * 512 + NW], ALU.add),
                              inc=("c_dv" if n == nd - 1 else None), settle=False)
                        p.wait("gpsimd", "c_dv")
                        p.dma("gpsimd",
                              s_v[tt * TT_ + g * 128: tt * TT_ + (g + 1) * 128, :],
                              gM[:, 0:D])
                    V(lambda e: e.engine_nop(), inc="c_dv")
                    evac_dv[ti] = p.now("c_dv")
            if fm:
                V(lambda e, pi=pi, NT=NT: e.tensor_reduce(
                    out=stat3[:, 48 + pi:49 + pi], in_=stat3[:, 32:32 + NT], axis=AX,
                    op=ALU.max), inc="c_dv", settle=True)
        qkv_spill = p.now("d_gpsimd")

        V(TT(stat3[:, 50:51], stat3[:, 48:49], stat3[:, 49:50], ALU.max),
          inc="c_dv")
        p.wait("tensor", "c_dv")
        TE(lambda e: e.transpose(P7[0:1, 0:128], stat3[:, 50:51], identf[:]),
           inc="c_pe")
        p.wait("vector", "c_pe")
        V(lambda e: e.tensor_reduce(out=gH[0:1, 0:1], in_=P7[0:1, 0:128], axis=AX,
                                    op=ALU.max), settle=True)
        V(TT(gH[0:1, 1:2], gH[0:1, 0:1], gH[0:1, 0:1], ALU.mult))
        V(TS2(gH[0:1, 1:2], gH[0:1, 1:2], 8.0, 80.0, ALU.mult, ALU.min))
        V(TS1(gH[0:1, 1:2], gH[0:1, 1:2], -1.0, ALU.mult), inc="c_dv")
        p.wait("gpsimd", "c_dv")
        p.dma("gpsimd", s_bc2[0:1], gH[0:1, 1:2])
        cb_dv = p.dma("gpsimd", csc[:, 4:5], s_bc2[0:1].broadcast_to((128, 1)))

        TNK = S // 128
        for buf in (gC, gD):
            V(lambda e, buf=buf: e.memset(
                buf[:, 0:TNK * 65].rearrange("p (k c) -> p k c", c=65)[:, :, 64:65],
                1.0), inc="c_dv")
        p.wait("scalar", "d_gpsimd", cb_dv)
        p.wait("sync", "c_dv")
        p.wait("sync", "c_pe")
        p.wait("sync", "d_gpsimd", qkv_spill)
        head_done = {}
        shv = sh16k[:].bitcast(F32)
        exp_done = {}
        av_free = {}
        gkc = 0
        for h in range(HEADS):
            vb = (gC, gD)[h % 2]
            if h % 2 == 0:
                kfm, qfm = gA, gB
            else:
                kfm, qfm = gF, shv
            if h >= 2:
                p.wait("sync", "c_pe", head_done[h - 2])
            dvk = p.dma("sync", kfm[0:64, 0:S], s_k[h * 64:(h + 1) * 64, :])
            dvq = p.dma("sync", qfm[0:64, 0:S], s_q[h * 64:(h + 1) * 64, :])
            dvv = p.dma("sync",
                        vb[:, 0:TNK * 65].rearrange("p (k c) -> p k c", c=65)[:, :, 0:64],
                        s_v.rearrange("(k p) d -> p k d", p=128)[:, :,
                                                                 h * 64:(h + 1) * 64])
            p.wait("tensor", "d_sync", dvv)
            for qt in range(QT):
                avp = PS[2] if qt % 2 == 0 else PS[5]
                gq = h * QT + qt
                avr = 2 if QT >= 2 else 1
                if gq - avr in av_free:
                    p.wait("tensor", "c_dv", av_free[gq - avr])
                def emit_av2(kc_a, gkc_a, last):
                    for hf in range(2):
                        ebp = gE[:, ((gkc_a % 2) * 2 + hf) * 256:
                                 ((gkc_a % 2) * 2 + hf) * 256 + 256]
                        p.wait("tensor", "c_sc", exp_done[(gkc_a, hf)])
                        TE(lambda e, vb=vb, avp=avp, kc_a=kc_a, ebp=ebp, hf=hf:
                           e.matmul(
                               avp[0:65, hf * 256:hf * 256 + 256],
                               vb[:, kc_a * 65:kc_a * 65 + 65], ebp,
                               start=(kc_a == 0 and hf == 0),
                               stop=(last and hf == 1)),
                           inc=("c_pe" if last and hf == 1 else None))

                for kc in range(TNK):
                    ps = PS[gkc % 2]
                    if (gkc - 2, 1) in exp_done:
                        p.wait("tensor", "c_sc", exp_done[(gkc - 2, 1)])
                    TE(lambda e, ps=ps, kfm=kfm, qfm=qfm, kc=kc, qt=qt: e.matmul(
                        ps[:, 0:512], kfm[0:64, kc * 128:(kc + 1) * 128],
                        qfm[0:64, qt * 512:(qt + 1) * 512], start=True, stop=True),
                       inc="c_pe")
                    qk_done = p.now("c_pe")
                    p.wait("scalar", "c_pe", qk_done)
                    for hf in range(2):
                        eb = gE[:, ((gkc % 2) * 2 + hf) * 256:
                                ((gkc % 2) * 2 + hf) * 256 + 256]
                        exp_done[(gkc, hf)] = SC(
                            lambda e, ps=ps, eb=eb, hf=hf: e.activation(
                                out=eb, in_=ps[:, hf * 256:hf * 256 + 256],
                                func=AF.Exp, bias=csc[:, 4:5],
                                scale=0.125), inc="c_sc")
                    if kc >= 1:
                        emit_av2(kc - 1, gkc - 1, False)
                    gkc += 1
                emit_av2(TNK - 1, gkc - 1, True)
                p.wait("vector", "c_pe")
                V(lambda e, avp=avp, qt=qt: e.tensor_copy(
                    sm_r[0:1, qt * 512:(qt + 1) * 512], avp[64:65, 0:512]),
                  inc="c_dv")
                p.wait("vector", "c_dv")
                V(lambda e, qt=qt: e.reciprocal(sm_r[0:1, qt * 512:(qt + 1) * 512],
                                                sm_r[0:1, qt * 512:(qt + 1) * 512]),
                  inc="c_dv", settle=True)
                p.wait("gpsimd", "c_dv")
                p.dma("gpsimd", s_bc[0:512], sm_r[0:1, qt * 512:(qt + 1) * 512])
                bc_dv = p.dma("gpsimd", gE[0:64, 1024:1536],
                              s_bc[None, 0:512].broadcast_to((64, 512)))
                p.wait("vector", "d_gpsimd", bc_dv)
                av_free[gq] = V(TT(gE[0:64, 1536:2048], avp[0:64, 0:512],
                                   gE[0:64, 1024:1536], ALU.mult), inc="c_dv",
                                settle=False)
                p.wait("gpsimd", "c_dv")
                p.dma("gpsimd", s_ao[h * 64:(h + 1) * 64, qt * 512:(qt + 1) * 512],
                      gE[0:64, 1536:2048])
            head_done[h] = p.now("c_pe")
        ao_done = p.now("d_gpsimd")

        pw_bufs = [gA, gB, gC, gD]
        p.wait("sync", "c_pe")
        p.wait("sync", "c_sc")
        for dc in range(DC):
            buf = pw_bufs[dc // 2]
            p.dma("sync", buf[:, (dc % 2) * 1024:(dc % 2) * 1024 + D],
                  wn["pw"].rearrange("(c p) d -> c p d", p=128)[dc])
        pw_dv = p.now("d_sync")
        proj_done = {}
        for g in range(TN):
            if g >= 1:
                p.wait("scalar", "c_pe", proj_done[g - 1])
            if g == 0:
                p.wait("scalar", "d_gpsimd", ao_done)
            dvh = p.dma("scalar",
                        gH[:, 0:DC * 128].rearrange("p (c u) -> p c u", c=DC),
                        s_ao.rearrange("(c p) s -> p c s", p=128)[:, :,
                                                                  g * 128:(g + 1) * 128])
            p.wait("tensor", "d_scalar", dvh)
            if g == 0:
                p.wait("tensor", "d_sync", pw_dv)
            NW = min(D, 512)
            for n in range(max(1, D // 512)):
                ps = PS[4 + n % 2]
                for dc in range(DC):
                    buf = pw_bufs[dc // 2]
                    TE(lambda e, ps=ps, buf=buf, dc=dc, n=n, NW=NW: e.matmul(
                        ps[:, 0:NW], gH[:, dc * 128:(dc + 1) * 128],
                        buf[:, (dc % 2) * 1024 + n * 512:
                            (dc % 2) * 1024 + n * 512 + NW],
                        start=(dc == 0), stop=(dc == DC - 1)),
                       inc=("c_pe" if dc == DC - 1 else None))
                p.wait("vector", "c_pe")
                xsl = x_tile(g)[:, n * 512:n * 512 + NW]
                V(TT(gM[:, 0:NW], ps[:, 0:NW], pbbc[:, n * 512:n * 512 + NW],
                     ALU.add), settle=False)
                V(TT(xsl, gM[:, 0:NW], xsl, ALU.add), inc="c_dv", settle=False)
                p.wait("tensor", "c_dv")
            proj_done[g] = p.now("c_pe")

    for seq in range(NSEQ):
        in_ap = x_h[seq].rearrange("(t p) d -> p t d", p=128)
        p.wait("gpsimd", "c_dv")
        xdv = p.dma("gpsimd", x_res[:].rearrange("p (t d) -> p t d", t=TN), in_ap)
        p.wait("vector", "d_gpsimd", xdv)
        p.wait("scalar", "d_gpsimd", xdv)
        if seq == 0:
            p.wait("scalar", "d_scalar", consts_dv)
            p.wait("vector", "d_scalar", consts_dv)
        for ph in phases:
            if ph == 'f':
                ffn(seq)
            else:
                attn(seq)
        p.wait("gpsimd", "c_dv")
        out_ap = o_h[seq].rearrange("(t p) d -> p t d", p=128)
        p.dma("gpsimd", out_ap, x_res[:].rearrange("p (t d) -> p t d", t=TN))

    for s in ("d_gpsimd", "d_sync", "d_scalar"):
        p.wait("sync", s)

    with nc.Block() as block:
        p.build(block)
    return nc, es


_CACHED = {}


def get_nc(NSEQ, S, D, HID, HEADS, phases='faf'):
    key = (NSEQ, S, D, HID, HEADS, phases)
    if key not in _CACHED:
        _CACHED[key] = build_kernel(NSEQ, S, D, HID, HEADS, phases)
    return _CACHED[key][0]


def kernel(L0_attn_x, L0_attn_q_w1, L0_attn_q_b1, L0_attn_q_w2, L0_attn_q_b2,
           L0_attn_k_w1, L0_attn_k_b1, L0_attn_k_w2, L0_attn_k_b2,
           L0_attn_v_w1, L0_attn_v_b1, L0_attn_v_w2, L0_attn_v_b2,
           L0_attn_proj_w, L0_attn_proj_b,
           L0_ffn_w1, L0_ffn_conv_w, L0_ffn_conv_b, L0_ffn_w2):
    x = np.ascontiguousarray(np.asarray(L0_attn_x, dtype=np.float32))
    B, S, D = x.shape
    HID = np.asarray(L0_ffn_w1).shape[0]
    HEADS = D // 64
    NSEQ = B // N_CORES
    nc = get_nc(NSEQ, S, D, HID, HEADS)
    w = dict(
        qw1=L0_attn_q_w1, qb1=L0_attn_q_b1, qw2=L0_attn_q_w2, qb2=L0_attn_q_b2,
        kw1=L0_attn_k_w1, kb1=L0_attn_k_b1, kw2=L0_attn_k_w2, kb2=L0_attn_k_b2,
        vw1=L0_attn_v_w1, vb1=L0_attn_v_b1, vw2=L0_attn_v_w2, vb2=L0_attn_v_b2,
        pw=L0_attn_proj_w, pb=L0_attn_proj_b,
        fw1=L0_ffn_w1, cw=L0_ffn_conv_w, cb=L0_ffn_conv_b, fw2=L0_ffn_w2)
    w = {k: np.ascontiguousarray(np.asarray(v, dtype=np.float32)) for k, v in w.items()}
    in_maps = [dict(w, x=x[i * NSEQ:(i + 1) * NSEQ]) for i in range(N_CORES)]
    res = run_bass_kernel_spmd(nc, in_maps, core_ids=list(range(N_CORES)))
    out = np.concatenate([r["out"] for r in res.results], axis=0)
    return out.astype(np.float32)
